# version 15
# baseline (speedup 1.0000x reference)
import contextlib
import numpy as np
import ml_dtypes
import concourse.bass as bass
import concourse.mybir as mybir
from concourse.bass_utils import run_bass_kernel_spmd

F32 = mybir.dt.float32
BF16 = mybir.dt.bfloat16
U8 = mybir.dt.uint8
AF = mybir.ActivationFunctionType
ALU = mybir.AluOpType
AX = mybir.AxisListType
BF = ml_dtypes.bfloat16

T = 4096
D = 1024
KC = 8
NH = 16
DH = 64
NTT = 32
NTAU = 8
I2 = 2816
NN = 22
SCALE0 = 1.0 / 8.0
NEGC = -12.0

ENGS = ("tensor", "vector", "scalar", "gpsimd", "sync")
ESEM = {"tensor": "pe", "vector": "dve", "scalar": "act", "gpsimd": "pool", "sync": "dsy"}
ALLSEMS = ("pe", "dve", "act", "pool", "dsy", "dgp")

POOL_BYTES = 208640


class Buf:
    __slots__ = ("w", "r")

    def __init__(self, barrier=()):
        self.w = None
        self.r = list(barrier)


class Rec:
    def __init__(self):
        self.q = {e: [] for e in ENGS}
        self.n = {}
        self.wm = {}

    def tok(self, sem):
        return (sem, self.n.get(sem, 0))

    def barrier(self):
        return [self.tok(s) for s in ALLSEMS]

    def _emit(self, eng, fn, waits, inc, k):
        ws = []
        for w in waits:
            if w is None:
                continue
            s, c = w
            if c <= 0:
                continue
            key = (eng, s)
            if self.wm.get(key, 0) < c:
                self.wm[key] = c
                ws.append((s, c))
        tok = None
        if inc is not None:
            self.n[inc] = self.n.get(inc, 0) + k
            tok = (inc, self.n[inc])
        self.q[eng].append((fn, ws, (inc, k) if inc else None))
        return tok

    def op(self, eng, fns, reads=(), writes=(), extra=(), dma=False):
        if callable(fns):
            fns = [fns]
        waits = list(extra)
        for bf in reads:
            waits.append(bf.w)
        for bf in writes:
            waits.append(bf.w)
            waits.extend(bf.r)
        sem = ("dgp" if eng == "gpsimd" else ("dsc" if eng == "scalar" else "dsy")) if dma else ESEM[eng]
        k = 16 if dma else 1
        tok = None
        for i, fn in enumerate(fns):
            last = i == len(fns) - 1
            inc = sem if (dma or last) else None
            tok = self._emit(eng, fn, waits if i == 0 else (), inc, k)
        for bf in reads:
            bf.r.append(tok)
        for bf in writes:
            bf.w = tok
            bf.r = []
        return tok

    def build(self, nc):
        with contextlib.ExitStack() as es:
            sems = {}
            for name in self.n:
                sems[name] = es.enter_context(nc.semaphore(name))
            with nc.Block() as block:
                def mk(eng_name):
                    def run(eng):
                        for fn, ws, inc in self.q[eng_name]:
                            for s, c in ws:
                                eng.wait_ge(sems[s], c)
                            instr = fn(eng)
                            if inc is not None:
                                instr.then_inc(sems[inc[0]], inc[1])
                    return run
                block.tensor(mk("tensor"))
                block.vector(mk("vector"))
                block.scalar(mk("scalar"))
                block.gpsimd(mk("gpsimd"))
                block.sync(mk("sync"))


def _fm(dram_ap):
    return dram_ap.rearrange("(c p) t -> p c t", p=128)


class Ctx:

    def __init__(self, nc, r, es):
        self.nc, self.r, self.es = nc, r, es
        E = es.enter_context
        self.ones_sb = E(nc.sbuf_tensor("ones_sb", [128, 512], BF16))
        self.onesf_sb = E(nc.sbuf_tensor("onesf_sb", [1, 512], F32))
        self.ident_sb = E(nc.sbuf_tensor("ident_sb", [128, 128], BF16))
        self.negc_sb = E(nc.sbuf_tensor("negc_sb", [128, 1], F32))
        self.epsc_sb = E(nc.sbuf_tensor("epsc_sb", [128, 1], F32))
        self.pool = E(nc.sbuf_tensor("pool", [128, POOL_BYTES], U8))
        self.pool_addr = nc.lookup_mloc(self.pool).addr
        self.PS = E(nc.psum_tensor("PS", [128, 8, 512], F32))
        self.bank = [Buf() for _ in range(8)]
        self.b_ones, self.b_onesf, self.b_ident = Buf(), Buf(), Buf()
        self.b_negc, self.b_eps = Buf(), Buf()

    def load_consts(self, cones, conesf, cident):
        r = self.r
        r.op("sync", lambda s: s.dma_start(out=self.ones_sb[:], in_=cones),
             writes=[self.b_ones], dma=True)
        r.op("sync", lambda s: s.dma_start(out=self.onesf_sb[:], in_=conesf),
             writes=[self.b_onesf], dma=True)
        r.op("sync", lambda s: s.dma_start(out=self.ident_sb[:], in_=cident),
             writes=[self.b_ident], dma=True)
        r.op("gpsimd", lambda g: g.memset(self.negc_sb[:], NEGC), writes=[self.b_negc])
        r.op("gpsimd", lambda g: g.memset(self.epsc_sb[:], 1e-5), writes=[self.b_eps])

    def carve(self, name, off, shape, dt=BF16):
        return self.nc.alloc_sbuf_tensor_at(name, shape, dt, offset=self.pool_addr + off)

    def psb(self, i):
        return self.PS[:, i, :]

    def psg(self, i, n):
        return self.PS[:, i:i + n, :].rearrange("p a b -> p (a b)")

    def psbf(self, i):
        return self.PS[:, i, :].bitcast(BF16)

    def psap(self, part0, nparts, bank, col0=0, ncol=512):
        return bass.AP(self.PS, part0 * 4096 + bank * 512 + col0,
                       [[4096, nparts], [1, ncol]])


def emit_ln(cx, hs, tau, b, src_fm, t_src_extra=()):
    r, PS, psb = cx.r, cx.PS, cx.psb
    sl = slice(tau * 512, (tau + 1) * 512)
    r.op("sync", lambda s, b=b, sl=sl: s.dma_start(
        out=hs["xt_buf"][:, b, :, :], in_=src_fm[:, :, sl]),
        writes=[hs["b_xt"][b]], extra=t_src_extra, dma=True)
    r.op("scalar", lambda s, b=b: s.activation(
        hs["sq_buf"][:], hs["xt_buf"][:, b, :, :], AF.Square),
        reads=[hs["b_xt"][b]], writes=[hs["b_sq"]])
    fns = [(lambda t, b=b, c=c: t.matmul(
        psb(6), cx.ones_sb[:, 0:128], hs["xt_buf"][:, b, c, :],
        start=(c == 0), stop=(c == KC - 1))) for c in range(KC)]
    r.op("tensor", fns, reads=[hs["b_xt"][b], cx.b_ones], writes=[cx.bank[6]])
    fns = [(lambda t, c=c: t.matmul(
        psb(7), cx.ones_sb[:, 0:128], hs["sq_buf"][:, c, :],
        start=(c == 0), stop=(c == KC - 1))) for c in range(KC)]
    r.op("tensor", fns, reads=[hs["b_sq"], cx.b_ones], writes=[cx.bank[7]])
    r.op("vector", lambda v: v.tensor_scalar_mul(hs["mbuf"][:], psb(6), 1.0 / D),
         reads=[cx.bank[6]], writes=[hs["b_m"]])
    r.op("vector", lambda v: v.tensor_mul(hs["msqb"][:], hs["mbuf"][:], hs["mbuf"][:]),
         reads=[hs["b_m"]], writes=[hs["b_msq"]])
    r.op("vector", lambda v: v.scalar_tensor_tensor(
        hs["varb"][:], psb(7), 1.0 / D, hs["msqb"][:], ALU.mult, ALU.subtract),
        reads=[cx.bank[7], hs["b_msq"]], writes=[hs["b_var"]])
    r.op("scalar", lambda s: s.activation(hs["msqb"][:], hs["varb"][:], AF.Ln,
                                          bias=cx.epsc_sb[:]),
         reads=[hs["b_var"], cx.b_eps], writes=[hs["b_msq"]])
    r.op("scalar", lambda s: s.activation(hs["varb"][:], hs["msqb"][:], AF.Exp, scale=-0.5),
         reads=[hs["b_msq"]], writes=[hs["b_var"]])
    r.op("vector", lambda v, b=b: v.tensor_tensor(
        hs["tf_buf"][:], hs["xt_buf"][:, b, :, :],
        bass.AP(hs["mbuf"], 0, [[512, 128], [0, KC], [1, 512]]), ALU.subtract),
        reads=[hs["b_xt"][b], hs["b_m"]], writes=[hs["b_tf"]])
    r.op("vector", lambda v, b=b: v.tensor_tensor(
        hs["xn_ap"](b), hs["tf_buf"][:],
        bass.AP(hs["varb"], 0, [[512, 128], [0, KC], [1, 512]]), ALU.mult),
        reads=[hs["b_tf"], hs["b_var"]], writes=[hs["b_xn"][b] if isinstance(hs["b_xn"], list) else hs["b_xn"]])


def mk_ln_bufs(cx, base, pfx, xn_dbl=True):
    carve = cx.carve
    bar = cx.r.barrier()
    hs = {}
    hs["xt_buf"] = carve(pfx + "xt", base + 0, [128, 2, KC, 512])
    hs["sq_buf"] = carve(pfx + "sq", base + 16384, [128, KC, 512])
    hs["tf_buf"] = carve(pfx + "tf", base + 24576, [128, KC, 512])
    xn = carve(pfx + "xn", base + 32768, [128, 2, KC, 512] if xn_dbl else [128, KC, 512])
    hs["xn_buf"] = xn
    hs["xn_ap"] = (lambda b: xn[:, b, :, :]) if xn_dbl else (lambda b: xn[:])
    hs["mbuf"] = carve(pfx + "mb", base + 49152, [128, 512], F32)
    hs["msqb"] = carve(pfx + "ms", base + 51200, [128, 512], F32)
    hs["varb"] = carve(pfx + "va", base + 53248, [128, 512], F32)
    hs["b_xt"] = [Buf(bar), Buf(bar)]
    hs["b_sq"], hs["b_tf"] = Buf(bar), Buf(bar)
    hs["b_xn"] = [Buf(bar), Buf(bar)] if xn_dbl else Buf(bar)
    hs["b_m"], hs["b_msq"], hs["b_var"] = Buf(bar), Buf(bar), Buf(bar)
    return hs


def emit_attn_phase(cx, d, pfx, PB, vsb, b_vsb, causal, mask_sb=None, b_mask=None,
                    t_qkdone=None):
    r = cx.r
    carve, psb, psap = cx.carve, cx.psb, cx.psap
    PS, bank = cx.PS, cx.bank
    B_ = Buf
    kh_buf = carve(pfx + "kh", PB + 0, [128, T])
    qh_buf = carve(pfx + "qh", PB + 16384, [128, 2, 512])
    ptil = carve(pfx + "ptil", PB + 18432, [128, 2, 2, 3, 512])
    rd_buf = carve(pfx + "rd", PB + 30720, [1, 1024], F32)
    rdb_s = carve(pfx + "rdb", PB + 34816, [128, 512], F32)
    attn_stage = carve(pfx + "ats", PB + 38912, [128, 2, 512])
    bar = r.barrier()
    b_kh = B_(bar)
    b_qh = [B_(bar), B_(bar)]
    b_ptil = [[B_(bar), B_(bar)], [B_(bar), B_(bar)]]
    b_rd = [B_(bar), B_(bar)]
    b_rdb = B_(bar)
    b_ats = [B_(bar), B_(bar)]

    for hp in range(NH // 2):
        h0 = hp * 2
        r.op("sync", lambda s, h0=h0: s.dma_start(
            out=kh_buf[:], in_=d["k_scr"][h0 * 64:(h0 + 2) * 64, :]),
            writes=[b_kh], extra=[t_qkdone], dma=True)
        for qt in range(NTAU):
            qb = qt % 2
            nkc = 4 * (qt + 1) if causal else NTT
            groups = [list(range(i, min(i + 3, nkc))) for i in range(0, nkc, 3)]
            ng = len(groups)
            r.op("sync", lambda s, h0=h0, qt=qt, qb=qb: s.dma_start(
                out=qh_buf[:, qb, :],
                in_=d["q_scr"][h0 * 64:(h0 + 2) * 64, qt * 512:(qt + 1) * 512]),
                writes=[b_qh[qb]], extra=[t_qkdone], dma=True)

            def s_group(hd, gi, grp):
                base = hd * 3
                p0 = hd * 64
                sl_ = gi % 2
                fns = [(lambda t, kc=kc, bk=base + kc - grp[0], p0=p0, qb=qb: t.matmul(
                    psb(bk),
                    bass.AP(kh_buf, p0 * T + kc * 128, [[T, 64], [1, 128]]),
                    bass.AP(qh_buf, p0 * 1024 + qb * 512, [[1024, 64], [1, 512]]),
                    start=True, stop=True)) for kc in grp]
                r.op("tensor", fns, reads=[b_kh, b_qh[qb]],
                     writes=[bank[base + i] for i in range(len(grp))])
                r.op("scalar", lambda s, n=len(grp), base=base, hd=hd, sl_=sl_: s.activation(
                    ptil[:, hd, sl_, 0:n, :].rearrange("p a b -> p (a b)"),
                    cx.psg(base, n), AF.Exp, bias=cx.negc_sb[:], scale=SCALE0),
                    reads=[bank[base + i] for i in range(len(grp))] + [cx.b_negc],
                    writes=[b_ptil[hd][sl_]])
                if causal and grp[-1] >= nkc - 4:
                    lo = max(grp[0], nkc - 4)
                    j0 = lo - grp[0]
                    i0m = lo - (nkc - 4)
                    nm = len(grp) - j0
                    r.op("vector", lambda v, j0=j0, nm=nm, hd=hd, sl_=sl_, i0m=i0m:
                         v.tensor_tensor(
                             ptil[:, hd, sl_, j0:j0 + nm, :],
                             ptil[:, hd, sl_, j0:j0 + nm, :],
                             mask_sb[:, i0m:i0m + nm, :], ALU.mult),
                         reads=[b_ptil[hd][sl_], b_mask], writes=[b_ptil[hd][sl_]])

            def av_group(hd, gi, grp):
                sl_ = gi % 2
                avb = 6 + hd
                h = h0 + hd
                fns = [(lambda t, kc=kc, j=kc - grp[0], h=h, sl_=sl_, hd=hd, avb=avb,
                        st=(kc == 0), sp=(kc == nkc - 1): t.matmul(
                    psap(0, 65, avb), vsb[:, kc, h * 65:h * 65 + 65],
                    ptil[:, hd, sl_, j, :], start=st, stop=sp,
                    skip_group_check=True)) for kc in grp]
                r.op("tensor", fns, reads=[b_vsb, b_ptil[hd][sl_]], writes=[bank[avb]])

            pend = []
            for gi, grp in enumerate(groups):
                s_group(0, gi, grp)
                s_group(1, gi, grp)
                pend.append((gi, grp))
                if len(pend) > 1:
                    g2 = pend.pop(0)
                    av_group(0, *g2)
                    av_group(1, *g2)
            for g2 in pend:
                av_group(0, *g2)
                av_group(1, *g2)

            for hd in range(2):
                avb = 6 + hd
                rb = hd * 3
                h = h0 + hd
                r.op("vector", lambda v, hd=hd, avb=avb: v.reciprocal(
                    rd_buf[:, hd * 512:(hd + 1) * 512], psap(64, 1, avb)),
                    reads=[bank[avb]], writes=[b_rd[hd]])
                r.op("tensor", lambda t, hd=hd, rb=rb: t.matmul(
                    psap(0, 64, rb), cx.onesf_sb[:, 0:64],
                    rd_buf[:, hd * 512:(hd + 1) * 512],
                    start=True, stop=True),
                    reads=[b_rd[hd], cx.b_onesf], writes=[bank[rb]])
                r.op("scalar", lambda s, hd=hd, rb=rb: s.activation(
                    bass.AP(rdb_s, hd * 64 * 512, [[512, 64], [1, 512]]),
                    psap(0, 64, rb), AF.Copy),
                    reads=[bank[rb]], writes=[b_rdb])
                r.op("vector", lambda v, hd=hd, qb=qb, avb=avb: v.tensor_tensor(
                    bass.AP(attn_stage, hd * 64 * 1024 + qb * 512,
                            [[1024, 64], [1, 512]]),
                    psap(0, 64, avb),
                    bass.AP(rdb_s, hd * 64 * 512, [[512, 64], [1, 512]]), ALU.mult),
                    reads=[bank[avb], b_rdb], writes=[b_ats[qb]])
            r.op("sync", lambda s, h0=h0, qt=qt, qb=qb: s.dma_start(
                out=d["attn_scr"][h0 * 64:(h0 + 2) * 64, qt * 512:(qt + 1) * 512],
                in_=attn_stage[:, qb, :]),
                reads=[b_ats[qb]], dma=True)


def emit_precast(cx, d):
    r = cx.r
    stg = cx.carve("pc_stg", 197632, [128, 2, 2048])
    bar = r.barrier()
    b_stg = [Buf(bar), Buf(bar)]
    pieces = []
    for e in range(8):
        for ic in range(32):
            pieces.append((d["w1"][e, ic], d["w1b"][e, ic]))
            pieces.append((d["w3"][e, ic], d["w3b"][e, ic]))
    for e in range(8):
        for hh in range(2):
            for q in range(8):
                pieces.append((d["w2"][e, hh][:, q * 2048:(q + 1) * 2048],
                               d["w2b"][e, hh][:, q * 2048:(q + 1) * 2048]))
    for i, (src_ap, dst_ap) in enumerate(pieces):
        b = i % 2
        ne = src_ap.shape[-1]
        r.op("gpsimd", lambda g, src_ap=src_ap, b=b, ne=ne: g.dma_start(
            out=stg[:, b, 0:ne], in_=src_ap), writes=[b_stg[b]], dma=True)
        r.op("sync", lambda s, dst_ap=dst_ap, b=b, ne=ne: s.dma_start(
            out=dst_ap, in_=stg[:, b, 0:ne]), reads=[b_stg[b]], dma=True)
    return r.tok("dsy")


def emit_l0attn(cx, d):
    nc, r = cx.nc, cx.r
    carve, psb, psbf, psap = cx.carve, cx.psb, cx.psbf, cx.psap
    PS, bank = cx.PS, cx.bank
    B_ = Buf

    wqkv_sb = carve("a_wqkv", 0, [128, KC, 3 * D])
    bqkv_sb = carve("a_bqkv", 49152, [1, 3 * D])
    wproj_sb = carve("a_wproj", 55296, [128, KC, D])
    bproj_sb = carve("a_bproj", 71680, [1, D])
    vsb = carve("a_vsb", 73728, [128, NTT, 1040])
    PB = 140288
    bar = r.barrier()
    b_wqkv, b_bqkv, b_wproj, b_bproj, b_vsb = B_(bar), B_(bar), B_(bar), B_(bar), B_(bar)

    r.op("gpsimd", lambda g: g.dma_start(
        out=wqkv_sb[:], in_=d["wqkv"].rearrange("(kc p) m -> p kc m", p=128)),
        writes=[b_wqkv], dma=True)
    r.op("gpsimd", lambda g: g.dma_start(out=bqkv_sb[:], in_=d["bqkv"]), writes=[b_bqkv], dma=True)
    r.op("gpsimd", lambda g: g.dma_start(
        out=wproj_sb[:], in_=d["wproj"].rearrange("(kc p) m -> p kc m", p=128)),
        writes=[b_wproj], dma=True)
    r.op("gpsimd", lambda g: g.dma_start(out=bproj_sb[:], in_=d["bproj"]), writes=[b_bproj], dma=True)
    r.op("gpsimd", lambda g: g.memset(
        bass.AP(vsb, 64, [[NTT * 1040, 128], [1040, NTT], [65, 16]]), 1.0), writes=[b_vsb])

    xtm_buf = carve("a_xtm", PB + 0, [128, 2, D])
    xstage = carve("a_xstage", PB + 4096, [128, 2, KC, 128])
    bar = r.barrier()
    b_xtm = [B_(bar), B_(bar)]
    b_xstage = [B_(bar), B_(bar)]
    for tt in range(NTT):
        b = tt % 2
        r.op("gpsimd", lambda g, b=b, tt=tt: g.dma_start(
            out=xtm_buf[:, b, :], in_=d["x"][tt * 128:(tt + 1) * 128, :]),
            writes=[b_xtm[b]], dma=True)
        fns = [(lambda t, b=b, c=c: t.transpose(
            psbf(b)[:, c * 128:(c + 1) * 128],
            xtm_buf[:, b, c * 128:(c + 1) * 128], cx.ident_sb[:])) for c in range(KC)]
        r.op("tensor", fns, reads=[b_xtm[b], cx.b_ident], writes=[bank[b]])
        r.op("vector", lambda v, b=b: v.tensor_copy(
            xstage[:, b, :, :], psbf(b).rearrange("p (c t) -> p c t", c=KC)),
            reads=[bank[b]], writes=[b_xstage[b]])
        r.op("sync", lambda s, b=b, tt=tt: s.dma_start(
            out=_fm(d["xfm_scr"])[:, :, tt * 128:(tt + 1) * 128], in_=xstage[:, b, :, :]),
            reads=[b_xstage[b]], dma=True)
    t_p0done = r.tok("dsy")

    hs = mk_ln_bufs(cx, PB, "a_")
    qk_stage = carve("a_qks", PB + 55296, [128, 2, 512])
    bar = r.barrier()
    b_qks = [B_(bar), B_(bar)]
    xn_buf = hs["xn_buf"]

    for tau in range(NTAU):
        b = tau % 2
        sl = slice(tau * 512, (tau + 1) * 512)
        emit_ln(cx, hs, tau, b, _fm(d["xfm_scr"]), t_src_extra=[t_p0done])
        for f in range(16):
            pb = f % 2
            fns = [(lambda t, b=b, c=c, f=f, pb=pb: t.matmul(
                psb(pb), wqkv_sb[:, c, f * 128:(f + 1) * 128], xn_buf[:, b, c, :],
                start=(c == 0), stop=False)) for c in range(KC)]
            fns.append(lambda t, f=f, pb=pb: t.matmul(
                psb(pb), bqkv_sb[:, f * 128:(f + 1) * 128], cx.ones_sb[0:1, :],
                start=False, stop=True))
            r.op("tensor", fns, reads=[hs["b_xn"][b], b_wqkv, b_bqkv, cx.b_ones],
                 writes=[bank[pb]])
            r.op("scalar", lambda s, f=f, pb=pb: s.activation(
                qk_stage[:, f % 2, :], psb(pb), AF.Copy),
                reads=[bank[pb]], writes=[b_qks[f % 2]])
            dst = d["q_scr"] if f < 8 else d["k_scr"]
            fr = f % 8
            r.op("sync", lambda s, fr=fr, dst=dst, sl=sl, f=f: s.dma_start(
                out=dst[fr * 128:(fr + 1) * 128, sl], in_=qk_stage[:, f % 2, :]),
                reads=[b_qks[f % 2]], dma=True)
        for sub in range(4):
            tt = tau * 4 + sub
            for j in range(2):
                pb = 2 + (sub * 2 + j) % 2
                fns = [(lambda t, b=b, c=c, sub=sub, j=j, pb=pb: t.matmul(
                    psb(pb), xn_buf[:, b, c, sub * 128:(sub + 1) * 128],
                    wqkv_sb[:, c, 2048 + j * 512:2048 + (j + 1) * 512],
                    start=(c == 0), stop=False)) for c in range(KC)]
                fns.append(lambda t, j=j, pb=pb: t.matmul(
                    psb(pb), cx.ones_sb[0:1, 0:128],
                    bqkv_sb[:, 2048 + j * 512:2048 + (j + 1) * 512],
                    start=False, stop=True))
                r.op("tensor", fns, reads=[hs["b_xn"][b], b_wqkv, b_bqkv, cx.b_ones],
                     writes=[bank[pb]])
                r.op("vector", lambda v, tt=tt, j=j, pb=pb: v.tensor_copy(
                    bass.AP(vsb, tt * 1040 + j * 8 * 65, [[NTT * 1040, 128], [65, 8], [1, 64]]),
                    bass.AP(PS, pb * 512, [[4096, 128], [64, 8], [1, 64]])),
                    reads=[bank[pb]], writes=[b_vsb])
    t_qkdone = r.tok("dsy")

    emit_attn_phase(cx, d, "a_", PB, vsb, b_vsb, causal=False, t_qkdone=t_qkdone)
    t_attdone = r.tok("dsy")

    at_buf = carve("a_at", PB + 0, [128, 2, KC, 512])
    xt4_buf = carve("a_xt4", PB + 16384, [128, 2, KC, 512])
    x1_stage = carve("a_x1s", PB + 32768, [128, 2, KC, 512])
    bar = r.barrier()
    b_at = [B_(bar), B_(bar)]
    b_xt4 = [B_(bar), B_(bar)]
    b_x1 = [B_(bar), B_(bar)]
    for tau in range(NTAU):
        b = tau % 2
        sl = slice(tau * 512, (tau + 1) * 512)
        r.op("sync", lambda s, b=b, sl=sl: s.dma_start(
            out=at_buf[:, b, :, :], in_=_fm(d["attn_scr"])[:, :, sl]),
            writes=[b_at[b]], extra=[t_attdone], dma=True)
        r.op("sync", lambda s, b=b, sl=sl: s.dma_start(
            out=xt4_buf[:, b, :, :], in_=_fm(d["xfm_scr"])[:, :, sl]),
            writes=[b_xt4[b]], dma=True)
        for f in range(KC):
            pb = f % 2
            fns = [(lambda t, b=b, c=c, f=f, pb=pb: t.matmul(
                psb(pb), wproj_sb[:, c, f * 128:(f + 1) * 128], at_buf[:, b, c, :],
                start=(c == 0), stop=False)) for c in range(KC)]
            fns.append(lambda t, f=f, pb=pb: t.matmul(
                psb(pb), bproj_sb[:, f * 128:(f + 1) * 128], cx.ones_sb[0:1, :],
                start=False, stop=True))
            r.op("tensor", fns, reads=[b_at[b], b_wproj, b_bproj, cx.b_ones], writes=[bank[pb]])
            r.op("vector", lambda v, b=b, f=f, pb=pb: v.tensor_tensor(
                x1_stage[:, b, f, :], xt4_buf[:, b, f, :], psb(pb), ALU.add),
                reads=[bank[pb], b_xt4[b]], writes=[b_x1[b]])
        r.op("sync", lambda s, b=b, sl=sl: s.dma_start(
            out=_fm(d["x1"])[:, :, sl], in_=x1_stage[:, b, :, :]),
            reads=[b_x1[b]], dma=True)


def emit_l0moe(cx, d, T_=T, I_=4096, NE=8, t_cast=None):
    NTAU_ = T_ // 512
    NTT_ = T_ // 128
    NIC = I_ // 128
    nc, r = cx.nc, cx.r
    carve, psb = cx.carve, cx.psb
    PS, bank = cx.PS, cx.bank
    B_ = Buf

    gate_sb = carve("m_gate", 207872, [128, KC, NE])
    wdb = carve("m_wdb", 208000, [128, NTT_, NE])
    bar = r.barrier()
    b_gate, b_wdb = B_(bar), B_(bar)

    r.op("gpsimd", lambda g: g.dma_start(
        out=gate_sb[:], in_=d["gate"].rearrange("(c p) e -> p c e", p=128)),
        writes=[b_gate], dma=True)

    hs = mk_ln_bufs(cx, 0, "m_")
    xn_buf = hs["xn_buf"]
    lgb = carve("m_lg", 55296, [128, NE], F32)
    lg2b = carve("m_lg2", 55296 + 32, [128, NE], F32)
    mk1b = carve("m_mk1", 55296 + 64, [128, NE], F32)
    mk2b = carve("m_mk2", 55296 + 96, [128, NE], F32)
    tmp8b = carve("m_t8", 55296 + 128, [128, NE], F32)
    m1b = carve("m_m1", 55296 + 160, [128, 1], F32)
    m2b = carve("m_m2", 55296 + 192, [128, 1], F32)
    ddb = carve("m_dd", 55296 + 224, [128, 1], F32)
    w1sb = carve("m_w1s", 55296 + 256, [128, 1], F32)
    w2sb_ = carve("m_w2s", 55296 + 288, [128, 1], F32)
    bar = r.barrier()
    b_lg, b_lg2, b_mk1, b_mk2, b_t8 = B_(bar), B_(bar), B_(bar), B_(bar), B_(bar)
    b_m1, b_m2, b_dd, b_w1s, b_w2s = B_(bar), B_(bar), B_(bar), B_(bar), B_(bar)

    for tau in range(NTAU_):
        b = tau % 2
        sl = slice(tau * 512, (tau + 1) * 512)
        emit_ln(cx, hs, tau, b, _fm(d["x1"]))
        r.op("sync", lambda s, b=b, sl=sl: s.dma_start(
            out=_fm(d["xn_scr"])[:, :, sl], in_=xn_buf[:, b, :, :]),
            reads=[hs["b_xn"][b]], dma=True)
        for st in range(4):
            TT = tau * 4 + st
            fns = [(lambda t, b=b, c=c, st=st: t.matmul(
                PS[:, 4, 0:NE], xn_buf[:, b, c, st * 128:(st + 1) * 128],
                gate_sb[:, c, :], start=(c == 0), stop=(c == KC - 1))) for c in range(KC)]
            r.op("tensor", fns, reads=[hs["b_xn"][b], b_gate], writes=[bank[4]])
            r.op("vector", lambda v: v.tensor_copy(lgb[:], PS[:, 4, 0:NE]),
                 reads=[bank[4]], writes=[b_lg])
            r.op("vector", lambda v: v.reduce_max(m1b[:], lgb[:], AX.X),
                 reads=[b_lg], writes=[b_m1])
            r.op("vector", lambda v: v.tensor_single_scalar(
                mk1b[:], lgb[:], m1b[:, 0:1], ALU.is_equal),
                reads=[b_lg, b_m1], writes=[b_mk1])
            r.op("vector", lambda v: v.scalar_tensor_tensor(
                lg2b[:], mk1b[:], -1e30, lgb[:], ALU.mult, ALU.add),
                reads=[b_mk1, b_lg], writes=[b_lg2])
            r.op("vector", lambda v: v.reduce_max(m2b[:], lg2b[:], AX.X),
                 reads=[b_lg2], writes=[b_m2])
            r.op("vector", lambda v: v.tensor_single_scalar(
                mk2b[:], lg2b[:], m2b[:, 0:1], ALU.is_equal),
                reads=[b_lg2, b_m2], writes=[b_mk2])
            r.op("vector", lambda v: v.tensor_sub(ddb[:], m1b[:], m2b[:]),
                 reads=[b_m1, b_m2], writes=[b_dd])
            r.op("scalar", lambda s: s.activation(w1sb[:], ddb[:], AF.Sigmoid),
                 reads=[b_dd], writes=[b_w1s])
            r.op("vector", lambda v: v.tensor_scalar(
                w2sb_[:], w1sb[:], -1.0, 1.0, ALU.mult, ALU.add),
                reads=[b_w1s], writes=[b_w2s])
            r.op("vector", lambda v: v.tensor_scalar(
                tmp8b[:], mk1b[:], w1sb[:, 0:1], None, ALU.mult),
                reads=[b_mk1, b_w1s], writes=[b_t8])
            r.op("vector", lambda v, TT=TT: v.scalar_tensor_tensor(
                wdb[:, TT, :], mk2b[:], w2sb_[:, 0:1], tmp8b[:], ALU.mult, ALU.add),
                reads=[b_mk2, b_w2s, b_t8], writes=[b_wdb])
    t_m0done = r.tok("dsy")

    xn1t = carve("m_xn1t", 0, [128, 2, KC, 512])
    x1t = carve("m_x1t", 16384, [128, KC, 512])
    x2st = carve("m_x2st", 24576, [128, KC, 512])
    hth = carve("m_hth", 32768, [128, NIC, 512])
    w1s_t = carve("m_w1s_t", 65536, [128, 2, KC, 128])
    w3s_t = carve("m_w3s_t", 69632, [128, 2, KC, 128])
    w2sb = carve("m_w2sb", 73728, [128, 2, NIC, 512])
    accb_f = carve("m_accf", 139264, [128, 4, 1024], F32)
    accb_h = carve("m_acch", 155648, [128, 4, 1024])
    sgb = carve("m_sg", 163840, [128, 512])
    htmp = carve("m_ht", 164864, [128, 512])
    wbc = carve("m_wbc", 165888, [128, 512])
    wdrep = carve("m_wdr", 166912, [128, 128])
    bar = r.barrier()
    b_xn1 = [B_(bar), B_(bar)]
    b_x1t, b_x2st = B_(bar), B_(bar)
    b_hth = B_(bar)
    b_w1t = [B_(bar), B_(bar)]
    b_w3t = [B_(bar), B_(bar)]
    b_w2t = [B_(bar), B_(bar)]
    b_acc, b_accb = B_(bar), B_(bar)
    b_sg, b_ht, b_wbc, b_wdr = B_(bar), B_(bar), B_(bar), B_(bar)

    for tau in range(NTAU_):
        sl = slice(tau * 512, (tau + 1) * 512)
        tb = tau % 2
        if tau == 0:
            r.op("sync", lambda s, sl=sl: s.dma_start(
                out=xn1t[:, 0, :, :], in_=_fm(d["xn_scr"])[:, :, sl]),
                writes=[b_xn1[0]], extra=[t_m0done], dma=True)
        if tau + 1 < NTAU_:
            r.op("sync", lambda s, tau=tau: s.dma_start(
                out=xn1t[:, (tau + 1) % 2, :, :],
                in_=_fm(d["xn_scr"])[:, :, (tau + 1) * 512:(tau + 2) * 512]),
                writes=[b_xn1[(tau + 1) % 2]], extra=[t_m0done], dma=True)
        for e in range(NE):
            for st in range(4):
                TT = tau * 4 + st
                r.op("vector", lambda v, TT=TT, e=e: v.tensor_copy(
                    wdrep[:], bass.AP(wdb, TT * NE + e, [[NTT_ * NE, 128], [0, 128]])),
                    reads=[b_wdb], writes=[b_wdr])
                r.op("tensor", lambda t, st=st: t.matmul(
                    bass.AP(PS, 6 * 512 + st * 128, [[4096, 128], [1, 128]]),
                    wdrep[:], cx.ident_sb[:], start=True, stop=True),
                    reads=[b_wdr, cx.b_ident], writes=[bank[6]])
            r.op("scalar", lambda s: s.activation(wbc[:], psb(6), AF.Copy),
                 reads=[bank[6]], writes=[b_wbc])
            for hh in range(2):
                r.op("sync", lambda s, e=e, hh=hh: s.dma_start(
                    out=w2sb[:, hh, :, :],
                    in_=d["w2b"][e, hh].rearrange("p (ic dd) -> p ic dd", ic=NIC)),
                    writes=[b_w2t[hh]], extra=[t_cast], dma=True)
            for ic in range(NIC):
                s_ = ic % 2
                if ic < 2:
                    for wsl, wt, bb in ((w1s_t, d["w1b"], b_w1t), (w3s_t, d["w3b"], b_w3t)):
                        r.op("scalar", lambda s, e=e, ic=ic, s_=s_, wsl=wsl, wt=wt: s.dma_start(
                            out=wsl[:, s_, :, :],
                            in_=wt[e, ic].rearrange("p (c i) -> p c i", c=KC)),
                            writes=[bb[s_]], extra=[t_cast], dma=True)
                bg, bu = s_ * 2, s_ * 2 + 1
                fns = [(lambda t, c=c, s_=s_, tb=tb: t.matmul(
                    PS[:, s_ * 2, :], w1s_t[:, s_, c, :], xn1t[:, tb, c, :],
                    start=(c == 0), stop=(c == KC - 1))) for c in range(KC)]
                r.op("tensor", fns, reads=[b_w1t[s_], b_xn1[tb]], writes=[bank[bg]])
                fns = [(lambda t, c=c, s_=s_, tb=tb: t.matmul(
                    PS[:, s_ * 2 + 1, :], w3s_t[:, s_, c, :], xn1t[:, tb, c, :],
                    start=(c == 0), stop=(c == KC - 1))) for c in range(KC)]
                r.op("tensor", fns, reads=[b_w3t[s_], b_xn1[tb]], writes=[bank[bu]])
                if ic + 2 < NIC:
                    for wsl, wt, bb in ((w1s_t, d["w1b"], b_w1t), (w3s_t, d["w3b"], b_w3t)):
                        r.op("scalar", lambda s, e=e, ic=ic, s_=s_, wsl=wsl, wt=wt: s.dma_start(
                            out=wsl[:, s_, :, :],
                            in_=wt[e, ic + 2].rearrange("p (c i) -> p c i", c=KC)),
                            writes=[bb[s_]], extra=[t_cast], dma=True)
                r.op("scalar", lambda s, bg=bg: s.activation(sgb[:], psb(bg), AF.Silu),
                     reads=[bank[bg]], writes=[b_sg])
                r.op("vector", lambda v, bu=bu: v.tensor_tensor(
                    htmp[:], sgb[:], psb(bu), ALU.mult),
                    reads=[b_sg, bank[bu]], writes=[b_ht])
                r.op("vector", lambda v, ic=ic: v.tensor_tensor(
                    hth[:, ic, :], htmp[:], wbc[:], ALU.mult),
                    reads=[b_ht, b_wbc], writes=[b_hth])
            for hh in range(2):
                for st in range(4):
                    pw = 4 + (st % 2)
                    fns = [(lambda t, ic=ic, st=st, hh=hh, pw=pw: t.matmul(
                        psb(pw), hth[:, ic, st * 128:(st + 1) * 128], w2sb[:, hh, ic, :],
                        start=(ic == 0), stop=(ic == NIC - 1))) for ic in range(NIC)]
                    r.op("tensor", fns, reads=[b_hth, b_w2t[hh]], writes=[bank[pw]])
                    if e == 0:
                        r.op("vector", lambda v, st=st, hh=hh, pw=pw: v.tensor_copy(
                            accb_f[:, st, hh * 512:(hh + 1) * 512], psb(pw)),
                            reads=[bank[pw]], writes=[b_acc])
                    else:
                        r.op("vector", lambda v, st=st, hh=hh, pw=pw: v.tensor_tensor(
                            accb_f[:, st, hh * 512:(hh + 1) * 512], psb(pw),
                            accb_f[:, st, hh * 512:(hh + 1) * 512], ALU.add),
                            reads=[bank[pw]], writes=[b_acc])
        r.op("vector", lambda v: v.tensor_copy(accb_h[:], accb_f[:]),
             reads=[b_acc], writes=[b_accb])
        r.op("sync", lambda s, sl=sl: s.dma_start(
            out=x1t[:], in_=_fm(d["x1"])[:, :, sl]), writes=[b_x1t], dma=True)
        for st in range(4):
            pt = 6 + (st % 2)
            fns = [(lambda t, st=st, dc=dc, pt=pt: t.transpose(
                PS[:, pt, :].bitcast(BF16)[:, dc * 128:(dc + 1) * 128],
                accb_h[:, st, dc * 128:(dc + 1) * 128], cx.ident_sb[:])) for dc in range(KC)]
            r.op("tensor", fns, reads=[b_accb, cx.b_ident], writes=[bank[pt]])
            r.op("vector", lambda v, st=st, pt=pt: v.tensor_tensor(
                x2st[:, :, st * 128:(st + 1) * 128], x1t[:, :, st * 128:(st + 1) * 128],
                PS[:, pt, :].bitcast(BF16).rearrange("p (c t) -> p c t", c=KC), ALU.add),
                reads=[bank[pt], b_x1t], writes=[b_x2st])
        r.op("sync", lambda s, sl=sl: s.dma_start(
            out=_fm(d["x2"])[:, :, sl], in_=x2st[:]),
            reads=[b_x2st], dma=True)


def emit_l1attn(cx, d):
    nc, r = cx.nc, cx.r
    carve, psb, psap = cx.carve, cx.psb, cx.psap
    PS, bank = cx.PS, cx.bank
    B_ = Buf

    wo_sb = carve("c_wo", 0, [128, KC, D])
    vsb = carve("c_vsb", 16384, [128, NTT, 1040])
    wq_sb = carve("c_wq", 82944, [128, 128])
    wk_sb = carve("c_wk", 83200, [128, 128])
    wv_sb = carve("c_wv", 83456, [128, 128])
    mask_sb = carve("c_mask", 83712, [128, 4, 512])
    PB = 87808
    bar = r.barrier()
    b_wq, b_wk, b_wv, b_wo, b_mask, b_vsb = B_(bar), B_(bar), B_(bar), B_(bar), B_(bar), B_(bar)

    r.op("gpsimd", lambda g: g.dma_start(out=wq_sb[:], in_=d["wq2"]), writes=[b_wq], dma=True)
    r.op("gpsimd", lambda g: g.dma_start(out=wk_sb[:], in_=d["wk2"]), writes=[b_wk], dma=True)
    r.op("gpsimd", lambda g: g.dma_start(out=wv_sb[:], in_=d["wv2"]), writes=[b_wv], dma=True)
    r.op("gpsimd", lambda g: g.dma_start(
        out=wo_sb[:], in_=d["wo"].rearrange("(kc p) m -> p kc m", p=128)), writes=[b_wo], dma=True)
    r.op("sync", lambda s: s.dma_start(
        out=mask_sb[:], in_=d["cmask"].rearrange("(i p) q -> p i q", p=128)),
        writes=[b_mask], dma=True)
    r.op("gpsimd", lambda g: g.memset(
        bass.AP(vsb, 64, [[NTT * 1040, 128], [1040, NTT], [65, 16]]), 1.0), writes=[b_vsb])

    hs = mk_ln_bufs(cx, PB, "c_")
    xn_buf = hs["xn_buf"]
    qk_stage = carve("c_qks", PB + 55296, [128, 2, 512])
    bar = r.barrier()
    b_qks = [B_(bar), B_(bar)]

    for tau in range(NTAU):
        b = tau % 2
        sl = slice(tau * 512, (tau + 1) * 512)
        emit_ln(cx, hs, tau, b, _fm(d["x2"]))
        for which, wsb, bw, dst in (("q", wq_sb, b_wq, d["q_scr"]), ("k", wk_sb, b_wk, d["k_scr"])):
            for c in range(KC):
                pb = c % 2
                r.op("tensor", lambda t, b=b, c=c, wsb=wsb, pb=pb: t.matmul(
                    psb(pb), wsb[:], xn_buf[:, b, c, :], start=True, stop=True),
                    reads=[hs["b_xn"][b], bw], writes=[bank[pb]])
                r.op("scalar", lambda s, pb=pb, c=c: s.activation(
                    qk_stage[:, c % 2, :], psb(pb), AF.Copy),
                    reads=[bank[pb]], writes=[b_qks[c % 2]])
                r.op("sync", lambda s, c=c, dst=dst, sl=sl: s.dma_start(
                    out=dst[c * 128:(c + 1) * 128, sl], in_=qk_stage[:, c % 2, :]),
                    reads=[b_qks[c % 2]], dma=True)
        for sub in range(4):
            tt = tau * 4 + sub
            for c in range(KC):
                pb = 2 + (c % 2)
                r.op("tensor", lambda t, b=b, c=c, sub=sub, pb=pb: t.matmul(
                    PS[:, pb, 0:128], xn_buf[:, b, c, sub * 128:(sub + 1) * 128], wv_sb[:],
                    start=True, stop=True),
                    reads=[hs["b_xn"][b], b_wv], writes=[bank[pb]])
                r.op("vector", lambda v, tt=tt, c=c, pb=pb: v.tensor_copy(
                    bass.AP(vsb, tt * 1040 + (2 * c) * 65, [[NTT * 1040, 128], [65, 2], [1, 64]]),
                    bass.AP(PS, pb * 512, [[4096, 128], [64, 2], [1, 64]])),
                    reads=[bank[pb]], writes=[b_vsb])
    t_qkdone = r.tok("dsy")

    emit_attn_phase(cx, d, "c_", PB, vsb, b_vsb, causal=True,
                    mask_sb=mask_sb, b_mask=b_mask, t_qkdone=t_qkdone)
    t_attdone = r.tok("dsy")

    at_buf = carve("c_at", PB + 0, [128, 2, KC, 512])
    xt4_buf = carve("c_xt4", PB + 16384, [128, 2, KC, 512])
    x3_stage = carve("c_x3s", PB + 32768, [128, 2, KC, 512])
    bar = r.barrier()
    b_at = [B_(bar), B_(bar)]
    b_xt4 = [B_(bar), B_(bar)]
    b_x3 = [B_(bar), B_(bar)]
    for tau in range(NTAU):
        b = tau % 2
        sl = slice(tau * 512, (tau + 1) * 512)
        r.op("sync", lambda s, b=b, sl=sl: s.dma_start(
            out=at_buf[:, b, :, :], in_=_fm(d["attn_scr"])[:, :, sl]),
            writes=[b_at[b]], extra=[t_attdone], dma=True)
        r.op("sync", lambda s, b=b, sl=sl: s.dma_start(
            out=xt4_buf[:, b, :, :], in_=_fm(d["x2"])[:, :, sl]),
            writes=[b_xt4[b]], dma=True)
        for f in range(KC):
            pb = f % 2
            fns = [(lambda t, b=b, c=c, f=f, pb=pb: t.matmul(
                psb(pb), wo_sb[:, c, f * 128:(f + 1) * 128], at_buf[:, b, c, :],
                start=(c == 0), stop=(c == KC - 1))) for c in range(KC)]
            r.op("tensor", fns, reads=[b_at[b], b_wo], writes=[bank[pb]])
            r.op("vector", lambda v, b=b, f=f, pb=pb: v.tensor_tensor(
                x3_stage[:, b, f, :], xt4_buf[:, b, f, :], psb(pb), ALU.add),
                reads=[bank[pb], b_xt4[b]], writes=[b_x3[b]])
        r.op("sync", lambda s, b=b, sl=sl: s.dma_start(
            out=_fm(d["x3"])[:, :, sl], in_=x3_stage[:, b, :, :]),
            reads=[b_x3[b]], dma=True)


def emit_l1ffn(cx, d):
    nc, r = cx.nc, cx.r
    carve, psb, psbf = cx.carve, cx.psb, cx.psbf
    PS, bank = cx.PS, cx.bank
    B_ = Buf
    U16 = mybir.dt.uint16

    wgT = carve("f_wgT", 0, [128, KC, I2])
    wuT = carve("f_wuT", 45056, [128, KC, I2])
    wdT = carve("f_wdT", 90112, [128, NN, D])
    FB = 135168
    bar = r.barrier()
    b_wgT, b_wuT, b_wdT = B_(bar), B_(bar), B_(bar)

    wp_buf = carve("f_wp", FB + 0, [128, 2816])
    ab_buf = carve("f_ab", FB + 5632, [128, 2816])
    accA = carve("f_accA", FB + 11264, [128, 4096])
    accB = carve("f_accB", FB + 19456, [128, 4096])
    amx = carve("f_amx", FB + 27648, [128, 176], F32)
    scb = carve("f_scb", FB + 28416, [128, 176], F32)
    rscb = carve("f_rscb", FB + 29184, [128, 176], F32)
    bar = r.barrier()
    b_wp, b_ab, b_aA, b_aB = B_(bar), B_(bar), B_(bar), B_(bar)
    b_amx, b_sc, b_rsc = B_(bar), B_(bar), B_(bar)

    U32 = mybir.dt.uint32

    def emit_quant(Xap, b_X, nblk):
        ne = nblk * 16
        X3 = Xap.rearrange("p (nb s) -> p nb s", s=16)
        r.op("vector", lambda v: v.tensor_reduce(
            amx[:, 0:nblk], X3, AX.X, ALU.max, apply_absolute_value=True),
            reads=[b_X], writes=[b_amx])
        r.op("vector", lambda v: v.tensor_scalar(
            scb[:, 0:nblk], amx[:, 0:nblk], 1.0 / 6.0, 1e-12, ALU.mult, ALU.max),
            reads=[b_amx], writes=[b_sc])
        r.op("vector", lambda v: v.reciprocal(rscb[:, 0:nblk], scb[:, 0:nblk]),
             reads=[b_sc], writes=[b_rsc])
        r.op("scalar", lambda s: s.activation(ab_buf[:, 0:ne], Xap, AF.Abs),
             reads=[b_X], writes=[b_ab])
        rsc_b = bass.AP(rscb, 0, [[176, 128], [1, nblk], [0, 16]])
        sc_b = bass.AP(scb, 0, [[176, 128], [1, nblk], [0, 16]])
        r.op("vector", lambda v: v.tensor_tensor(
            ab_buf[:, 0:ne].rearrange("p (nb s) -> p nb s", s=16),
            ab_buf[:, 0:ne].rearrange("p (nb s) -> p nb s", s=16), rsc_b, ALU.mult),
            reads=[b_rsc], writes=[b_ab])
        a = ab_buf[:, 0:ne]
        r.op("vector", lambda v: v.tensor_single_scalar(
            accA[:, 0:ne], a, 0.25, ALU.is_gt), reads=[b_ab], writes=[b_aA])
        for th in (0.75, 1.25, 1.75):
            r.op("vector", lambda v, th=th: v.scalar_tensor_tensor(
                accA[:, 0:ne], a, th, accA[:, 0:ne], ALU.is_gt, ALU.add),
                reads=[b_ab], writes=[b_aA])
        r.op("vector", lambda v: v.tensor_single_scalar(
            accB[:, 0:ne], a, 2.5, ALU.is_gt), reads=[b_ab], writes=[b_aB])
        for th in (3.5, 5.0, 5.0):
            r.op("vector", lambda v, th=th: v.scalar_tensor_tensor(
                accB[:, 0:ne], a, th, accB[:, 0:ne], ALU.is_gt, ALU.add),
                reads=[b_ab], writes=[b_aB])
        r.op("vector", lambda v: v.scalar_tensor_tensor(
            accA[:, 0:ne], accA[:, 0:ne], 0.5, accB[:, 0:ne], ALU.mult, ALU.add),
            reads=[b_aB], writes=[b_aA])
        r.op("vector", lambda v: v.tensor_tensor(
            accA[:, 0:ne].rearrange("p (nb s) -> p nb s", s=16),
            accA[:, 0:ne].rearrange("p (nb s) -> p nb s", s=16), sc_b, ALU.mult),
            reads=[b_sc], writes=[b_aA])
        r.op("vector", lambda v: v.tensor_single_scalar(
            ab_buf[:, 0:ne].bitcast(U32), Xap.bitcast(U32), -2147450880,
            ALU.bitwise_and),
            reads=[b_X], writes=[b_ab])
        r.op("vector", lambda v: v.tensor_tensor(
            Xap.bitcast(U32), accA[:, 0:ne].bitcast(U32), ab_buf[:, 0:ne].bitcast(U32),
            ALU.bitwise_or),
            reads=[b_aA, b_ab], writes=[b_X])

    for w_in, wT, b_wT in ((d["wg"], wgT, b_wgT), (d["wu"], wuT, b_wuT)):
        for pp in range(NN // 2):
            r.op("gpsimd", lambda g, w_in=w_in, pp=pp: g.dma_start(
                out=wp_buf[:, 0:2048].rearrange("p (n k) -> p n k", n=2),
                in_=w_in.rearrange("(n p) k -> p n k", p=128)[:, 2 * pp:2 * pp + 2, :]),
                writes=[b_wp], dma=True)
            emit_quant(wp_buf[:, 0:2048], b_wp, 128)
            for n2 in range(2):
                pt = 6 + n2
                fns = [(lambda t, n2=n2, kcc=kcc, pt=pt: t.transpose(
                    psbf(pt)[:, kcc * 128:(kcc + 1) * 128],
                    wp_buf[:, n2 * 1024 + kcc * 128:n2 * 1024 + (kcc + 1) * 128],
                    cx.ident_sb[:])) for kcc in range(KC)]
                r.op("tensor", fns, reads=[b_wp, cx.b_ident], writes=[bank[pt]])
                r.op("vector", lambda v, n2=n2, pp=pp, pt=pt, wT=wT: v.tensor_copy(
                    wT[:, :, (2 * pp + n2) * 128:(2 * pp + n2 + 1) * 128],
                    psbf(pt).rearrange("p (c t) -> p c t", c=KC)),
                    reads=[bank[pt]], writes=[b_wT])
    for dc in range(KC):
        r.op("gpsimd", lambda g, dc=dc: g.dma_start(
            out=wp_buf[:],
            in_=d["wd"].rearrange("(dd p) k -> p dd k", p=128)[:, dc, :]),
            writes=[b_wp], dma=True)
        emit_quant(wp_buf[:], b_wp, 176)
        for grp, n0 in ((8, 0), (8, 8), (6, 16)):
            pt = 6 + (n0 // 8) % 2
            fns = [(lambda t, icc=icc, pt=pt: t.transpose(
                psbf(pt)[:, (icc % 8) * 128:((icc % 8) + 1) * 128],
                wp_buf[:, icc * 128:(icc + 1) * 128],
                cx.ident_sb[:])) for icc in range(n0, n0 + grp)]
            r.op("tensor", fns, reads=[b_wp, cx.b_ident], writes=[bank[pt]])
            r.op("vector", lambda v, grp=grp, n0=n0, dc=dc, pt=pt: v.tensor_copy(
                wdT[:, n0:n0 + grp, dc * 128:(dc + 1) * 128],
                psbf(pt).rearrange("p (c t) -> p c t", c=KC)[:, 0:grp, :]),
                reads=[bank[pt]], writes=[b_wdT])

    sg_buf = carve("f_sg", FB + 29952, [128, 512])
    xt_buf = carve("f_xt", FB + 30976, [128, 1, KC, 512])
    h_tm1 = carve("f_htm1", FB + 39168, [128, 2816])
    xn_buf = carve("f_xn", FB + 44800, [128, KC, 512])
    hq_fm = carve("f_hq", FB + 44800, [128, NN, 128])
    sq_buf = carve("f_sq", FB + 11264, [128, KC, 512])
    tf_buf = carve("f_tf", FB + 19456, [128, KC, 512])
    mbuf = carve("f_mb", FB + 52992, [128, 512], F32)
    msqb = carve("f_ms", FB + 55040, [128, 512], F32)
    varb = carve("f_va", FB + 57088, [128, 512], F32)
    xtm_sb = carve("f_xtm", FB + 59136, [128, 2048])
    xq_fm = carve("f_xq", FB + 63232, [128, KC, 512])
    x4_sub = carve("f_x4", FB + 71424, [128, KC, 128])
    out_tm = carve("f_otm", FB + 52992, [128, 1024], F32)
    bar = r.barrier()
    b_m = B_(bar)
    b_xn = B_(bar)
    b_xt = B_(bar)
    hs = {
        "xt_buf": xt_buf,
        "sq_buf": sq_buf, "tf_buf": tf_buf,
        "xn_buf": xn_buf, "xn_ap": (lambda b: xn_buf[:]),
        "mbuf": mbuf, "msqb": msqb, "varb": varb,
        "b_xt": [b_xt, b_xt], "b_sq": b_aA, "b_tf": b_aB,
        "b_xn": b_xn, "b_m": b_m, "b_msq": b_m, "b_var": B_(bar),
    }
    b_xtm, b_xq = B_(bar), B_(bar)
    b_htm = [b_wp, B_(bar)]
    h_tm = [wp_buf, h_tm1]
    b_hqfm = b_xn
    b_sg = B_(bar)
    b_x4 = B_(bar)
    b_otm = b_m

    def gate_up(sub, tb):
        for ns in range(6):
            n0 = ns * 512
            nw = min(512, I2 - n0)
            ga = (ns % 2) * 2
            fns = [(lambda t, c=c, n0=n0, nw=nw, ga=ga, sub=sub: t.matmul(
                PS[:, ga, 0:nw], xq_fm[:, c, sub * 128:(sub + 1) * 128],
                wgT[:, c, n0:n0 + nw],
                start=(c == 0), stop=(c == KC - 1))) for c in range(KC)]
            r.op("tensor", fns, reads=[b_xq, b_wgT], writes=[bank[ga]])
            fns = [(lambda t, c=c, n0=n0, nw=nw, ga=ga, sub=sub: t.matmul(
                PS[:, ga + 1, 0:nw], xq_fm[:, c, sub * 128:(sub + 1) * 128],
                wuT[:, c, n0:n0 + nw],
                start=(c == 0), stop=(c == KC - 1))) for c in range(KC)]
            r.op("tensor", fns, reads=[b_xq, b_wuT], writes=[bank[ga + 1]])
            r.op("scalar", lambda s, nw=nw, ga=ga: s.activation(
                sg_buf[:, 0:nw], PS[:, ga, 0:nw], AF.Silu),
                reads=[bank[ga]], writes=[b_sg])
            r.op("vector", lambda v, n0=n0, nw=nw, ga=ga, tb=tb: v.tensor_tensor(
                h_tm[tb][:, n0:n0 + nw], sg_buf[:, 0:nw], PS[:, ga + 1, 0:nw], ALU.mult),
                reads=[b_sg, bank[ga + 1]], writes=[b_htm[tb]])

    def tail(sub, tb, tau):
        emit_quant(h_tm[tb][:], b_htm[tb], 176)
        for grp, n0 in ((8, 0), (8, 8), (6, 16)):
            pt = 6 + (n0 // 8) % 2
            fns = [(lambda t, icc=icc, pt=pt, tb=tb: t.transpose(
                psbf(pt)[:, (icc % 8) * 128:((icc % 8) + 1) * 128],
                h_tm[tb][:, icc * 128:(icc + 1) * 128], cx.ident_sb[:]))
                for icc in range(n0, n0 + grp)]
            r.op("tensor", fns, reads=[b_htm[tb], cx.b_ident], writes=[bank[pt]])
            r.op("scalar", lambda s, grp=grp, n0=n0, pt=pt: s.activation(
                hq_fm[:, n0:n0 + grp, :],
                psbf(pt).rearrange("p (c t) -> p c t", c=KC)[:, 0:grp, :], AF.Copy),
                reads=[bank[pt]], writes=[b_hqfm])
        for f in range(KC):
            pw = 4 + (f // 4)
            col = (f % 4) * 128
            fns = [(lambda t, icc=icc, f=f, pw=pw, col=col: t.matmul(
                bass.AP(PS, pw * 512 + col, [[4096, 128], [1, 128]]),
                wdT[:, icc, f * 128:(f + 1) * 128], hq_fm[:, icc, :],
                start=(icc == 0), stop=(icc == NN - 1))) for icc in range(NN)]
            r.op("tensor", fns, reads=[b_wdT, b_hqfm], writes=[bank[pw]])
        r.op("vector", lambda v, sub=sub: v.tensor_tensor(
            x4_sub[:], xt_buf[:, 0, :, sub * 128:(sub + 1) * 128],
            PS[:, 4:6, :].rearrange("p a b -> p (a b)").rearrange(
                "p (c t) -> p c t", c=KC), ALU.add),
            reads=[bank[4], bank[5], b_xt], writes=[b_x4])
        fns = [(lambda t, c=c: t.transpose(
            psbf(6)[:, c * 128:(c + 1) * 128],
            x4_sub[:, c, :], cx.ident_sb[:])) for c in range(KC)]
        r.op("tensor", fns, reads=[b_x4, cx.b_ident], writes=[bank[6]])
        r.op("scalar", lambda s: s.activation(out_tm[:], psbf(6), AF.Copy),
             reads=[bank[6]], writes=[b_otm])
        r.op("sync", lambda s, tau=tau, sub=sub: s.dma_start(
            out=d["out"][tau * 512 + sub * 128:tau * 512 + (sub + 1) * 128, :],
            in_=out_tm[:]),
            reads=[b_otm], dma=True)

    for tau in range(NTAU):
        emit_ln(cx, hs, tau, 0, _fm(d["x3"]))
        for hf in range(2):
            for s2 in range(2):
                sub = hf * 2 + s2
                fns = [(lambda t, c=c, sub=sub: t.transpose(
                    psbf(7)[:, c * 128:(c + 1) * 128],
                    xn_buf[:, c, sub * 128:(sub + 1) * 128], cx.ident_sb[:]))
                    for c in range(KC)]
                r.op("tensor", fns, reads=[b_xn, cx.b_ident], writes=[bank[7]])
                r.op("scalar", lambda s, s2=s2: s.activation(
                    xtm_sb[:, s2 * 1024:(s2 + 1) * 1024], psbf(7), AF.Copy),
                    reads=[bank[7]], writes=[b_xtm])
            emit_quant(xtm_sb[:], b_xtm, 128)
            for s2 in range(2):
                sub = hf * 2 + s2
                fns = [(lambda t, c=c, s2=s2: t.transpose(
                    psbf(7)[:, c * 128:(c + 1) * 128],
                    xtm_sb[:, s2 * 1024 + c * 128:s2 * 1024 + (c + 1) * 128],
                    cx.ident_sb[:])) for c in range(KC)]
                r.op("tensor", fns, reads=[b_xtm, cx.b_ident], writes=[bank[7]])
                r.op("scalar", lambda s, sub=sub: s.activation(
                    bass.AP(xq_fm, sub * 128, [[KC * 512, 128], [512, KC], [1, 128]]),
                    psbf(7).rearrange("p (c t) -> p c t", c=KC), AF.Copy),
                    reads=[bank[7]], writes=[b_xq])
        gate_up(0, 0)
        for sub in range(1, 4):
            gate_up(sub, sub % 2)
            tail(sub - 1, (sub - 1) % 2, tau)
        tail(3, 1, tau)

def build_fused():
    nc = bass.Bass("TRN2", target_bir_lowering=False)
    d = {}
    d["x"] = nc.dram_tensor("x", [T, D], F32, kind="ExternalInput").ap()
    d["wqkv"] = nc.dram_tensor("wqkv", [D, 3 * D], F32, kind="ExternalInput").ap()
    d["bqkv"] = nc.dram_tensor("bqkv", [1, 3 * D], F32, kind="ExternalInput").ap()
    d["wproj"] = nc.dram_tensor("wproj", [D, D], F32, kind="ExternalInput").ap()
    d["bproj"] = nc.dram_tensor("bproj", [1, D], F32, kind="ExternalInput").ap()
    d["gate"] = nc.dram_tensor("gate", [D, 8], F32, kind="ExternalInput").ap()
    d["w1"] = nc.dram_tensor("w1", [8, 32, 128, 1024], F32, kind="ExternalInput").ap()
    d["w3"] = nc.dram_tensor("w3", [8, 32, 128, 1024], F32, kind="ExternalInput").ap()
    d["w2"] = nc.dram_tensor("w2", [8, 2, 128, 32 * 512], F32, kind="ExternalInput").ap()
    d["wq2"] = nc.dram_tensor("wq2", [128, 128], F32, kind="ExternalInput").ap()
    d["wk2"] = nc.dram_tensor("wk2", [128, 128], F32, kind="ExternalInput").ap()
    d["wv2"] = nc.dram_tensor("wv2", [128, 128], F32, kind="ExternalInput").ap()
    d["wo"] = nc.dram_tensor("wo", [D, D], F32, kind="ExternalInput").ap()
    d["wg"] = nc.dram_tensor("wg", [I2, D], F32, kind="ExternalInput").ap()
    d["wu"] = nc.dram_tensor("wu", [I2, D], F32, kind="ExternalInput").ap()
    d["wd"] = nc.dram_tensor("wd", [D, I2], F32, kind="ExternalInput").ap()
    d["cones"] = nc.dram_tensor("cones", [128, 512], BF16, kind="ExternalInput").ap()
    d["conesf"] = nc.dram_tensor("conesf", [1, 512], F32, kind="ExternalInput").ap()
    d["cident"] = nc.dram_tensor("cident", [128, 128], BF16, kind="ExternalInput").ap()
    d["cmask"] = nc.dram_tensor("cmask", [512, 512], BF16, kind="ExternalInput").ap()

    d["w1b"] = nc.dram_tensor("w1b", [8, 32, 128, 1024], BF16).ap()
    d["w3b"] = nc.dram_tensor("w3b", [8, 32, 128, 1024], BF16).ap()
    d["w2b"] = nc.dram_tensor("w2b", [8, 2, 128, 32 * 512], BF16).ap()
    d["xfm_scr"] = nc.dram_tensor("xfm_scr", [D, T], BF16).ap()
    d["q_scr"] = nc.dram_tensor("q_scr", [D, T], BF16).ap()
    d["k_scr"] = nc.dram_tensor("k_scr", [D, T], BF16).ap()
    d["attn_scr"] = nc.dram_tensor("attn_scr", [D, T], BF16).ap()
    d["x1"] = nc.dram_tensor("x1", [D, T], BF16).ap()
    d["xn_scr"] = nc.dram_tensor("xn_scr", [D, T], BF16).ap()
    d["x2"] = nc.dram_tensor("x2", [D, T], BF16).ap()
    d["x3"] = nc.dram_tensor("x3", [D, T], BF16).ap()
    d["out"] = nc.dram_tensor("out", [T, D], F32, kind="ExternalOutput").ap()

    r = Rec()
    with contextlib.ExitStack() as es:
        cx = Ctx(nc, r, es)
        cx.load_consts(d["cones"], d["conesf"], d["cident"])
        t_cast = emit_precast(cx, d)
        emit_l0attn(cx, d)
        emit_l0moe(cx, d, t_cast=t_cast)
        emit_l1attn(cx, d)
        emit_l1ffn(cx, d)
        r._emit("sync", lambda s: s.nop(), [r.tok("dsy"), r.tok("dgp")], None, 1)
        r.build(nc)
    return nc


_cache = {}


def _consts():
    return dict(
        cones=np.ones((128, 512), dtype=BF),
        conesf=np.ones((1, 512), dtype=np.float32),
        cident=np.eye(128, dtype=BF),
    )


def _blkdiag(W):
    z = np.zeros((128, 128), np.float32)
    z[0:64, 0:64] = W.T
    z[64:128, 64:128] = W.T
    return z


def _causal_mask():
    j = np.arange(512)[None, :]
    m = np.zeros((512, 512), np.float32)
    for i in range(4):
        p = np.arange(128)[:, None]
        m[i * 128:(i + 1) * 128, :] = (j >= p + 128 * i)
    return m.astype(BF)


def kernel(**inputs):
    x = np.asarray(inputs["L0_attn_x"], dtype=np.float32)
    c = _consts()
    if "fused" not in _cache:
        _cache["fused"] = build_fused()
    common = dict(
        wqkv=np.ascontiguousarray(inputs["L0_attn_w_qkv"], dtype=np.float32),
        bqkv=np.ascontiguousarray(inputs["L0_attn_b_qkv"], dtype=np.float32).reshape(1, -1),
        wproj=np.ascontiguousarray(inputs["L0_attn_w_proj"], dtype=np.float32),
        bproj=np.ascontiguousarray(inputs["L0_attn_b_proj"], dtype=np.float32).reshape(1, -1),
        gate=np.ascontiguousarray(inputs["L0_ffn_gate"], dtype=np.float32),
        w1=np.ascontiguousarray(
            np.asarray(inputs["L0_ffn_experts_w1"], np.float32)
            .reshape(8, 8, 128, 32, 128).transpose(0, 3, 2, 1, 4).reshape(8, 32, 128, 1024)),
        w3=np.ascontiguousarray(
            np.asarray(inputs["L0_ffn_experts_w3"], np.float32)
            .reshape(8, 8, 128, 32, 128).transpose(0, 3, 2, 1, 4).reshape(8, 32, 128, 1024)),
        w2=np.ascontiguousarray(
            np.asarray(inputs["L0_ffn_experts_w2"], np.float32)
            .reshape(8, 32, 128, 2, 512).transpose(0, 3, 2, 1, 4).reshape(8, 2, 128, 32 * 512)),
        wq2=_blkdiag(np.asarray(inputs["L1_attn_Wq"], np.float32)),
        wk2=_blkdiag(np.asarray(inputs["L1_attn_Wk"], np.float32)),
        wv2=_blkdiag(np.asarray(inputs["L1_attn_Wv"], np.float32)),
        wo=np.ascontiguousarray(np.asarray(inputs["L1_attn_Wo"], np.float32).T),
        wg=np.ascontiguousarray(inputs["L1_ffn_w_gate"], dtype=np.float32),
        wu=np.ascontiguousarray(inputs["L1_ffn_w_up"], dtype=np.float32),
        wd=np.ascontiguousarray(inputs["L1_ffn_w_down"], dtype=np.float32),
        cmask=_causal_mask(),
        **c,
    )
    in_maps = [dict(x=np.ascontiguousarray(x[core]), **common) for core in range(8)]
    res = run_bass_kernel_spmd(_cache["fused"], in_maps, core_ids=list(range(8)))
    return np.stack([rr["out"] for rr in res.results])


# version 16 (final)
# speedup vs baseline: 1.1586x; 1.1586x over previous
import contextlib
import numpy as np
import ml_dtypes
import concourse.bass as bass
import concourse.mybir as mybir
from concourse.bass_utils import run_bass_kernel_spmd

F32 = mybir.dt.float32
BF16 = mybir.dt.bfloat16
U8 = mybir.dt.uint8
AF = mybir.ActivationFunctionType
ALU = mybir.AluOpType
AX = mybir.AxisListType
BF = ml_dtypes.bfloat16

T = 4096
D = 1024
KC = 8
NH = 16
DH = 64
NTT = 32
NTAU = 8
I2 = 2816
NN = 22
SCALE0 = 1.0 / 8.0
NEGC = -12.0

ENGS = ("tensor", "vector", "scalar", "gpsimd", "sync")
ESEM = {"tensor": "pe", "vector": "dve", "scalar": "act", "gpsimd": "pool", "sync": "dsy"}
ALLSEMS = ("pe", "dve", "act", "pool", "dsy", "dgp")

POOL_BYTES = 208640


class Buf:
    __slots__ = ("w", "r")

    def __init__(self, barrier=()):
        self.w = None
        self.r = list(barrier)


class Rec:
    def __init__(self):
        self.q = {e: [] for e in ENGS}
        self.n = {}
        self.wm = {}

    def tok(self, sem):
        return (sem, self.n.get(sem, 0))

    def barrier(self):
        return [self.tok(s) for s in ALLSEMS]

    def _emit(self, eng, fn, waits, inc, k):
        ws = []
        for w in waits:
            if w is None:
                continue
            s, c = w
            if c <= 0:
                continue
            key = (eng, s)
            if self.wm.get(key, 0) < c:
                self.wm[key] = c
                ws.append((s, c))
        tok = None
        if inc is not None:
            self.n[inc] = self.n.get(inc, 0) + k
            tok = (inc, self.n[inc])
        self.q[eng].append((fn, ws, (inc, k) if inc else None))
        return tok

    def op(self, eng, fns, reads=(), writes=(), extra=(), dma=False):
        if callable(fns):
            fns = [fns]
        waits = list(extra)
        for bf in reads:
            waits.append(bf.w)
        for bf in writes:
            waits.append(bf.w)
            waits.extend(bf.r)
        sem = ("dgp" if eng == "gpsimd" else "dsy") if dma else ESEM[eng]
        k = 16 if dma else 1
        tok = None
        for i, fn in enumerate(fns):
            last = i == len(fns) - 1
            inc = sem if (dma or last) else None
            tok = self._emit(eng, fn, waits if i == 0 else (), inc, k)
        for bf in reads:
            bf.r.append(tok)
        for bf in writes:
            bf.w = tok
            bf.r = []
        return tok

    def build(self, nc):
        with contextlib.ExitStack() as es:
            sems = {}
            for name in self.n:
                sems[name] = es.enter_context(nc.semaphore(name))
            with nc.Block() as block:
                def mk(eng_name):
                    def run(eng):
                        for fn, ws, inc in self.q[eng_name]:
                            for s, c in ws:
                                eng.wait_ge(sems[s], c)
                            instr = fn(eng)
                            if inc is not None:
                                instr.then_inc(sems[inc[0]], inc[1])
                    return run
                block.tensor(mk("tensor"))
                block.vector(mk("vector"))
                block.scalar(mk("scalar"))
                block.gpsimd(mk("gpsimd"))
                block.sync(mk("sync"))


def _fm(dram_ap):
    return dram_ap.rearrange("(c p) t -> p c t", p=128)


class Ctx:

    def __init__(self, nc, r, es):
        self.nc, self.r, self.es = nc, r, es
        E = es.enter_context
        self.ones_sb = E(nc.sbuf_tensor("ones_sb", [128, 512], BF16))
        self.onesf_sb = E(nc.sbuf_tensor("onesf_sb", [1, 512], F32))
        self.ident_sb = E(nc.sbuf_tensor("ident_sb", [128, 128], BF16))
        self.negc_sb = E(nc.sbuf_tensor("negc_sb", [128, 1], F32))
        self.epsc_sb = E(nc.sbuf_tensor("epsc_sb", [128, 1], F32))
        self.pool = E(nc.sbuf_tensor("pool", [128, POOL_BYTES], U8))
        self.pool_addr = nc.lookup_mloc(self.pool).addr
        self.PS = E(nc.psum_tensor("PS", [128, 8, 512], F32))
        self.bank = [Buf() for _ in range(8)]
        self.b_ones, self.b_onesf, self.b_ident = Buf(), Buf(), Buf()
        self.b_negc, self.b_eps = Buf(), Buf()

    def load_consts(self, cones, conesf, cident):
        r = self.r
        r.op("sync", lambda s: s.dma_start(out=self.ones_sb[:], in_=cones),
             writes=[self.b_ones], dma=True)
        r.op("sync", lambda s: s.dma_start(out=self.onesf_sb[:], in_=conesf),
             writes=[self.b_onesf], dma=True)
        r.op("sync", lambda s: s.dma_start(out=self.ident_sb[:], in_=cident),
             writes=[self.b_ident], dma=True)
        r.op("gpsimd", lambda g: g.memset(self.negc_sb[:], NEGC), writes=[self.b_negc])
        r.op("gpsimd", lambda g: g.memset(self.epsc_sb[:], 1e-5), writes=[self.b_eps])

    def carve(self, name, off, shape, dt=BF16):
        return self.nc.alloc_sbuf_tensor_at(name, shape, dt, offset=self.pool_addr + off)

    def psb(self, i):
        return self.PS[:, i, :]

    def psg(self, i, n):
        return self.PS[:, i:i + n, :].rearrange("p a b -> p (a b)")

    def psbf(self, i):
        return self.PS[:, i, :].bitcast(BF16)

    def psap(self, part0, nparts, bank, col0=0, ncol=512):
        return bass.AP(self.PS, part0 * 4096 + bank * 512 + col0,
                       [[4096, nparts], [1, ncol]])


def emit_ln(cx, hs, tau, b, src_fm, t_src_extra=()):
    r, PS, psb = cx.r, cx.PS, cx.psb
    sl = slice(tau * 512, (tau + 1) * 512)
    r.op("sync", lambda s, b=b, sl=sl: s.dma_start(
        out=hs["xt_buf"][:, b, :, :], in_=src_fm[:, :, sl]),
        writes=[hs["b_xt"][b]], extra=t_src_extra, dma=True)
    r.op("scalar", lambda s, b=b: s.activation(
        hs["sq_buf"][:], hs["xt_buf"][:, b, :, :], AF.Square),
        reads=[hs["b_xt"][b]], writes=[hs["b_sq"]])
    fns = [(lambda t, b=b, c=c: t.matmul(
        psb(6), cx.ones_sb[:, 0:128], hs["xt_buf"][:, b, c, :],
        start=(c == 0), stop=(c == KC - 1))) for c in range(KC)]
    r.op("tensor", fns, reads=[hs["b_xt"][b], cx.b_ones], writes=[cx.bank[6]])
    fns = [(lambda t, c=c: t.matmul(
        psb(7), cx.ones_sb[:, 0:128], hs["sq_buf"][:, c, :],
        start=(c == 0), stop=(c == KC - 1))) for c in range(KC)]
    r.op("tensor", fns, reads=[hs["b_sq"], cx.b_ones], writes=[cx.bank[7]])
    r.op("vector", lambda v: v.tensor_scalar_mul(hs["mbuf"][:], psb(6), 1.0 / D),
         reads=[cx.bank[6]], writes=[hs["b_m"]])
    r.op("vector", lambda v: v.tensor_mul(hs["msqb"][:], hs["mbuf"][:], hs["mbuf"][:]),
         reads=[hs["b_m"]], writes=[hs["b_msq"]])
    r.op("vector", lambda v: v.scalar_tensor_tensor(
        hs["varb"][:], psb(7), 1.0 / D, hs["msqb"][:], ALU.mult, ALU.subtract),
        reads=[cx.bank[7], hs["b_msq"]], writes=[hs["b_var"]])
    r.op("scalar", lambda s: s.activation(hs["msqb"][:], hs["varb"][:], AF.Ln,
                                          bias=cx.epsc_sb[:]),
         reads=[hs["b_var"], cx.b_eps], writes=[hs["b_msq"]])
    r.op("scalar", lambda s: s.activation(hs["varb"][:], hs["msqb"][:], AF.Exp, scale=-0.5),
         reads=[hs["b_msq"]], writes=[hs["b_var"]])
    r.op("vector", lambda v, b=b: v.tensor_tensor(
        hs["tf_buf"][:], hs["xt_buf"][:, b, :, :],
        bass.AP(hs["mbuf"], 0, [[512, 128], [0, KC], [1, 512]]), ALU.subtract),
        reads=[hs["b_xt"][b], hs["b_m"]], writes=[hs["b_tf"]])
    r.op("vector", lambda v, b=b: v.tensor_tensor(
        hs["xn_ap"](b), hs["tf_buf"][:],
        bass.AP(hs["varb"], 0, [[512, 128], [0, KC], [1, 512]]), ALU.mult),
        reads=[hs["b_tf"], hs["b_var"]], writes=[hs["b_xn"][b] if isinstance(hs["b_xn"], list) else hs["b_xn"]])


def mk_ln_bufs(cx, base, pfx, xn_dbl=True):
    carve = cx.carve
    bar = cx.r.barrier()
    hs = {}
    hs["xt_buf"] = carve(pfx + "xt", base + 0, [128, 2, KC, 512])
    hs["sq_buf"] = carve(pfx + "sq", base + 16384, [128, KC, 512])
    hs["tf_buf"] = carve(pfx + "tf", base + 24576, [128, KC, 512])
    xn = carve(pfx + "xn", base + 32768, [128, 2, KC, 512] if xn_dbl else [128, KC, 512])
    hs["xn_buf"] = xn
    hs["xn_ap"] = (lambda b: xn[:, b, :, :]) if xn_dbl else (lambda b: xn[:])
    hs["mbuf"] = carve(pfx + "mb", base + 49152, [128, 512], F32)
    hs["msqb"] = carve(pfx + "ms", base + 51200, [128, 512], F32)
    hs["varb"] = carve(pfx + "va", base + 53248, [128, 512], F32)
    hs["b_xt"] = [Buf(bar), Buf(bar)]
    hs["b_sq"], hs["b_tf"] = Buf(bar), Buf(bar)
    hs["b_xn"] = [Buf(bar), Buf(bar)] if xn_dbl else Buf(bar)
    hs["b_m"], hs["b_msq"], hs["b_var"] = Buf(bar), Buf(bar), Buf(bar)
    return hs


def emit_attn_phase(cx, d, pfx, PB, vsb, b_vsb, causal, mask_sb=None, b_mask=None,
                    t_qkdone=None):
    r = cx.r
    carve, psb, psap = cx.carve, cx.psb, cx.psap
    PS, bank = cx.PS, cx.bank
    B_ = Buf
    kh_buf = carve(pfx + "kh", PB + 0, [128, T])
    qh_buf = carve(pfx + "qh", PB + 16384, [128, 2, 512])
    ptil = carve(pfx + "ptil", PB + 18432, [128, 2, 2, 3, 512])
    rd_buf = carve(pfx + "rd", PB + 30720, [1, 1024], F32)
    rdb_s = carve(pfx + "rdb", PB + 34816, [128, 512], F32)
    attn_stage = carve(pfx + "ats", PB + 38912, [128, 2, 512])
    bar = r.barrier()
    b_kh = B_(bar)
    b_qh = [B_(bar), B_(bar)]
    b_ptil = [[B_(bar), B_(bar)], [B_(bar), B_(bar)]]
    b_rd = [B_(bar), B_(bar)]
    b_rdb = B_(bar)
    b_ats = [B_(bar), B_(bar)]

    for hp in range(NH // 2):
        h0 = hp * 2
        r.op("sync", lambda s, h0=h0: s.dma_start(
            out=kh_buf[:], in_=d["k_scr"][h0 * 64:(h0 + 2) * 64, :]),
            writes=[b_kh], extra=[t_qkdone], dma=True)
        for qt in range(NTAU):
            qb = qt % 2
            nkc = 4 * (qt + 1) if causal else NTT
            groups = [list(range(i, min(i + 3, nkc))) for i in range(0, nkc, 3)]
            ng = len(groups)
            r.op("sync", lambda s, h0=h0, qt=qt, qb=qb: s.dma_start(
                out=qh_buf[:, qb, :],
                in_=d["q_scr"][h0 * 64:(h0 + 2) * 64, qt * 512:(qt + 1) * 512]),
                writes=[b_qh[qb]], extra=[t_qkdone], dma=True)

            def s_group(hd, gi, grp):
                base = hd * 3
                p0 = hd * 64
                sl_ = gi % 2
                fns = [(lambda t, kc=kc, bk=base + kc - grp[0], p0=p0, qb=qb: t.matmul(
                    psb(bk),
                    bass.AP(kh_buf, p0 * T + kc * 128, [[T, 64], [1, 128]]),
                    bass.AP(qh_buf, p0 * 1024 + qb * 512, [[1024, 64], [1, 512]]),
                    start=True, stop=True)) for kc in grp]
                r.op("tensor", fns, reads=[b_kh, b_qh[qb]],
                     writes=[bank[base + i] for i in range(len(grp))])
                r.op("scalar", lambda s, n=len(grp), base=base, hd=hd, sl_=sl_: s.activation(
                    ptil[:, hd, sl_, 0:n, :].rearrange("p a b -> p (a b)"),
                    cx.psg(base, n), AF.Exp, bias=cx.negc_sb[:], scale=SCALE0),
                    reads=[bank[base + i] for i in range(len(grp))] + [cx.b_negc],
                    writes=[b_ptil[hd][sl_]])
                if causal and grp[-1] >= nkc - 4:
                    lo = max(grp[0], nkc - 4)
                    j0 = lo - grp[0]
                    i0m = lo - (nkc - 4)
                    nm = len(grp) - j0
                    r.op("vector", lambda v, j0=j0, nm=nm, hd=hd, sl_=sl_, i0m=i0m:
                         v.tensor_tensor(
                             ptil[:, hd, sl_, j0:j0 + nm, :],
                             ptil[:, hd, sl_, j0:j0 + nm, :],
                             mask_sb[:, i0m:i0m + nm, :], ALU.mult),
                         reads=[b_ptil[hd][sl_], b_mask], writes=[b_ptil[hd][sl_]])

            def av_group(hd, gi, grp):
                sl_ = gi % 2
                avb = 6 + hd
                h = h0 + hd
                fns = [(lambda t, kc=kc, j=kc - grp[0], h=h, sl_=sl_, hd=hd, avb=avb,
                        st=(kc == 0), sp=(kc == nkc - 1): t.matmul(
                    psap(0, 128, avb), vsb[:, kc, h * 65:h * 65 + 128],
                    ptil[:, hd, sl_, j, :], start=st, stop=sp,
                    skip_group_check=True)) for kc in grp]
                r.op("tensor", fns, reads=[b_vsb, b_ptil[hd][sl_]], writes=[bank[avb]])

            pend = []
            for gi, grp in enumerate(groups):
                s_group(0, gi, grp)
                s_group(1, gi, grp)
                pend.append((gi, grp))
                if len(pend) > 1:
                    g2 = pend.pop(0)
                    av_group(0, *g2)
                    av_group(1, *g2)
            for g2 in pend:
                av_group(0, *g2)
                av_group(1, *g2)

            for hd in range(2):
                avb = 6 + hd
                rb = hd * 3
                h = h0 + hd
                r.op("vector", lambda v, hd=hd, avb=avb: v.reciprocal(
                    rd_buf[:, hd * 512:(hd + 1) * 512], psap(64, 1, avb)),
                    reads=[bank[avb]], writes=[b_rd[hd]])
                r.op("tensor", lambda t, hd=hd, rb=rb: t.matmul(
                    psap(0, 64, rb), cx.onesf_sb[:, 0:64],
                    rd_buf[:, hd * 512:(hd + 1) * 512],
                    start=True, stop=True),
                    reads=[b_rd[hd], cx.b_onesf], writes=[bank[rb]])
                r.op("scalar", lambda s, hd=hd, rb=rb: s.activation(
                    bass.AP(rdb_s, hd * 64 * 512, [[512, 64], [1, 512]]),
                    psap(0, 64, rb), AF.Copy),
                    reads=[bank[rb]], writes=[b_rdb])
                r.op("vector", lambda v, hd=hd, qb=qb, avb=avb: v.tensor_tensor(
                    bass.AP(attn_stage, hd * 64 * 1024 + qb * 512,
                            [[1024, 64], [1, 512]]),
                    psap(0, 64, avb),
                    bass.AP(rdb_s, hd * 64 * 512, [[512, 64], [1, 512]]), ALU.mult),
                    reads=[bank[avb], b_rdb], writes=[b_ats[qb]])
            r.op("sync", lambda s, h0=h0, qt=qt, qb=qb: s.dma_start(
                out=d["attn_scr"][h0 * 64:(h0 + 2) * 64, qt * 512:(qt + 1) * 512],
                in_=attn_stage[:, qb, :]),
                reads=[b_ats[qb]], dma=True)


def emit_precast(cx, d):
    r = cx.r
    stg = cx.carve("pc_stg", 197632, [128, 2, 2048])
    bar = r.barrier()
    b_stg = [Buf(bar), Buf(bar)]
    pieces = []
    for e in range(8):
        for ic in range(32):
            pieces.append((d["w1"][e, ic], d["w1b"][e, ic]))
            pieces.append((d["w3"][e, ic], d["w3b"][e, ic]))
    for e in range(8):
        for hh in range(2):
            for q in range(8):
                pieces.append((d["w2"][e, hh][:, q * 2048:(q + 1) * 2048],
                               d["w2b"][e, hh][:, q * 2048:(q + 1) * 2048]))
    for i, (src_ap, dst_ap) in enumerate(pieces):
        b = i % 2
        ne = src_ap.shape[-1]
        r.op("gpsimd", lambda g, src_ap=src_ap, b=b, ne=ne: g.dma_start(
            out=stg[:, b, 0:ne], in_=src_ap), writes=[b_stg[b]], dma=True)
        r.op("sync", lambda s, dst_ap=dst_ap, b=b, ne=ne: s.dma_start(
            out=dst_ap, in_=stg[:, b, 0:ne]), reads=[b_stg[b]], dma=True)
    return r.tok("dsy")


def emit_l0attn(cx, d):
    nc, r = cx.nc, cx.r
    carve, psb, psbf, psap = cx.carve, cx.psb, cx.psbf, cx.psap
    PS, bank = cx.PS, cx.bank
    B_ = Buf

    wqkv_sb = carve("a_wqkv", 0, [128, KC, 3 * D])
    bqkv_sb = carve("a_bqkv", 49152, [1, 3 * D])
    wproj_sb = carve("a_wproj", 55296, [128, KC, D])
    bproj_sb = carve("a_bproj", 71680, [1, D])
    vsb = carve("a_vsb", 73728, [128, NTT, 1168])
    PB = 148480
    bar = r.barrier()
    b_wqkv, b_bqkv, b_wproj, b_bproj, b_vsb = B_(bar), B_(bar), B_(bar), B_(bar), B_(bar)

    r.op("gpsimd", lambda g: g.dma_start(
        out=wqkv_sb[:], in_=d["wqkv"].rearrange("(kc p) m -> p kc m", p=128)),
        writes=[b_wqkv], dma=True)
    r.op("gpsimd", lambda g: g.dma_start(out=bqkv_sb[:], in_=d["bqkv"]), writes=[b_bqkv], dma=True)
    r.op("gpsimd", lambda g: g.dma_start(
        out=wproj_sb[:], in_=d["wproj"].rearrange("(kc p) m -> p kc m", p=128)),
        writes=[b_wproj], dma=True)
    r.op("gpsimd", lambda g: g.dma_start(out=bproj_sb[:], in_=d["bproj"]), writes=[b_bproj], dma=True)
    r.op("gpsimd", lambda g: g.memset(
        bass.AP(vsb, 64, [[NTT * 1168, 128], [1168, NTT], [65, 16]]), 1.0), writes=[b_vsb])
    r.op("gpsimd", lambda g: g.memset(
        bass.AP(vsb, 1040, [[NTT * 1168, 128], [1168, NTT], [1, 128]]), 0.0), writes=[b_vsb])

    xtm_buf = carve("a_xtm", PB + 0, [128, 2, D])
    xstage = carve("a_xstage", PB + 4096, [128, 2, KC, 128])
    bar = r.barrier()
    b_xtm = [B_(bar), B_(bar)]
    b_xstage = [B_(bar), B_(bar)]
    for tt in range(NTT):
        b = tt % 2
        r.op("gpsimd", lambda g, b=b, tt=tt: g.dma_start(
            out=xtm_buf[:, b, :], in_=d["x"][tt * 128:(tt + 1) * 128, :]),
            writes=[b_xtm[b]], dma=True)
        fns = [(lambda t, b=b, c=c: t.transpose(
            psbf(b)[:, c * 128:(c + 1) * 128],
            xtm_buf[:, b, c * 128:(c + 1) * 128], cx.ident_sb[:])) for c in range(KC)]
        r.op("tensor", fns, reads=[b_xtm[b], cx.b_ident], writes=[bank[b]])
        r.op("vector", lambda v, b=b: v.tensor_copy(
            xstage[:, b, :, :], psbf(b).rearrange("p (c t) -> p c t", c=KC)),
            reads=[bank[b]], writes=[b_xstage[b]])
        r.op("sync", lambda s, b=b, tt=tt: s.dma_start(
            out=_fm(d["xfm_scr"])[:, :, tt * 128:(tt + 1) * 128], in_=xstage[:, b, :, :]),
            reads=[b_xstage[b]], dma=True)
    t_p0done = r.tok("dsy")

    hs = mk_ln_bufs(cx, PB, "a_")
    qk_stage = carve("a_qks", PB + 55296, [128, 2, 512])
    bar = r.barrier()
    b_qks = [B_(bar), B_(bar)]
    xn_buf = hs["xn_buf"]

    for tau in range(NTAU):
        b = tau % 2
        sl = slice(tau * 512, (tau + 1) * 512)
        emit_ln(cx, hs, tau, b, _fm(d["xfm_scr"]), t_src_extra=[t_p0done])
        for f in range(16):
            pb = f % 2
            fns = [(lambda t, b=b, c=c, f=f, pb=pb: t.matmul(
                psb(pb), wqkv_sb[:, c, f * 128:(f + 1) * 128], xn_buf[:, b, c, :],
                start=(c == 0), stop=False)) for c in range(KC)]
            fns.append(lambda t, f=f, pb=pb: t.matmul(
                psb(pb), bqkv_sb[:, f * 128:(f + 1) * 128], cx.ones_sb[0:1, :],
                start=False, stop=True))
            r.op("tensor", fns, reads=[hs["b_xn"][b], b_wqkv, b_bqkv, cx.b_ones],
                 writes=[bank[pb]])
            r.op("scalar", lambda s, f=f, pb=pb: s.activation(
                qk_stage[:, f % 2, :], psb(pb), AF.Copy),
                reads=[bank[pb]], writes=[b_qks[f % 2]])
            dst = d["q_scr"] if f < 8 else d["k_scr"]
            fr = f % 8
            r.op("sync", lambda s, fr=fr, dst=dst, sl=sl, f=f: s.dma_start(
                out=dst[fr * 128:(fr + 1) * 128, sl], in_=qk_stage[:, f % 2, :]),
                reads=[b_qks[f % 2]], dma=True)
        for sub in range(4):
            tt = tau * 4 + sub
            for j in range(2):
                pb = 2 + (sub * 2 + j) % 2
                fns = [(lambda t, b=b, c=c, sub=sub, j=j, pb=pb: t.matmul(
                    psb(pb), xn_buf[:, b, c, sub * 128:(sub + 1) * 128],
                    wqkv_sb[:, c, 2048 + j * 512:2048 + (j + 1) * 512],
                    start=(c == 0), stop=False)) for c in range(KC)]
                fns.append(lambda t, j=j, pb=pb: t.matmul(
                    psb(pb), cx.ones_sb[0:1, 0:128],
                    bqkv_sb[:, 2048 + j * 512:2048 + (j + 1) * 512],
                    start=False, stop=True))
                r.op("tensor", fns, reads=[hs["b_xn"][b], b_wqkv, b_bqkv, cx.b_ones],
                     writes=[bank[pb]])
                r.op("vector", lambda v, tt=tt, j=j, pb=pb: v.tensor_copy(
                    bass.AP(vsb, tt * 1168 + j * 8 * 65, [[NTT * 1168, 128], [65, 8], [1, 64]]),
                    bass.AP(PS, pb * 512, [[4096, 128], [64, 8], [1, 64]])),
                    reads=[bank[pb]], writes=[b_vsb])
    t_qkdone = r.tok("dsy")

    emit_attn_phase(cx, d, "a_", PB, vsb, b_vsb, causal=False, t_qkdone=t_qkdone)
    t_attdone = r.tok("dsy")

    at_buf = carve("a_at", PB + 0, [128, 2, KC, 512])
    xt4_buf = carve("a_xt4", PB + 16384, [128, 2, KC, 512])
    x1_stage = carve("a_x1s", PB + 32768, [128, 2, KC, 512])
    bar = r.barrier()
    b_at = [B_(bar), B_(bar)]
    b_xt4 = [B_(bar), B_(bar)]
    b_x1 = [B_(bar), B_(bar)]
    for tau in range(NTAU):
        b = tau % 2
        sl = slice(tau * 512, (tau + 1) * 512)
        r.op("sync", lambda s, b=b, sl=sl: s.dma_start(
            out=at_buf[:, b, :, :], in_=_fm(d["attn_scr"])[:, :, sl]),
            writes=[b_at[b]], extra=[t_attdone], dma=True)
        r.op("sync", lambda s, b=b, sl=sl: s.dma_start(
            out=xt4_buf[:, b, :, :], in_=_fm(d["xfm_scr"])[:, :, sl]),
            writes=[b_xt4[b]], dma=True)
        for f in range(KC):
            pb = f % 2
            fns = [(lambda t, b=b, c=c, f=f, pb=pb: t.matmul(
                psb(pb), wproj_sb[:, c, f * 128:(f + 1) * 128], at_buf[:, b, c, :],
                start=(c == 0), stop=False)) for c in range(KC)]
            fns.append(lambda t, f=f, pb=pb: t.matmul(
                psb(pb), bproj_sb[:, f * 128:(f + 1) * 128], cx.ones_sb[0:1, :],
                start=False, stop=True))
            r.op("tensor", fns, reads=[b_at[b], b_wproj, b_bproj, cx.b_ones], writes=[bank[pb]])
            r.op("vector", lambda v, b=b, f=f, pb=pb: v.tensor_tensor(
                x1_stage[:, b, f, :], xt4_buf[:, b, f, :], psb(pb), ALU.add),
                reads=[bank[pb], b_xt4[b]], writes=[b_x1[b]])
        r.op("sync", lambda s, b=b, sl=sl: s.dma_start(
            out=_fm(d["x1"])[:, :, sl], in_=x1_stage[:, b, :, :]),
            reads=[b_x1[b]], dma=True)


def emit_l0moe(cx, d, T_=T, I_=4096, NE=8, t_cast=None):
    NTAU_ = T_ // 512
    NTT_ = T_ // 128
    NIC = I_ // 128
    nc, r = cx.nc, cx.r
    carve, psb = cx.carve, cx.psb
    PS, bank = cx.PS, cx.bank
    B_ = Buf

    gate_sb = carve("m_gate", 207872, [128, KC, NE])
    wdb = carve("m_wdb", 208000, [128, NTT_, NE])
    bar = r.barrier()
    b_gate, b_wdb = B_(bar), B_(bar)

    r.op("gpsimd", lambda g: g.dma_start(
        out=gate_sb[:], in_=d["gate"].rearrange("(c p) e -> p c e", p=128)),
        writes=[b_gate], dma=True)

    hs = mk_ln_bufs(cx, 0, "m_")
    xn_buf = hs["xn_buf"]
    lgb = carve("m_lg", 55296, [128, NE], F32)
    lg2b = carve("m_lg2", 55296 + 32, [128, NE], F32)
    mk1b = carve("m_mk1", 55296 + 64, [128, NE], F32)
    mk2b = carve("m_mk2", 55296 + 96, [128, NE], F32)
    tmp8b = carve("m_t8", 55296 + 128, [128, NE], F32)
    m1b = carve("m_m1", 55296 + 160, [128, 1], F32)
    m2b = carve("m_m2", 55296 + 192, [128, 1], F32)
    ddb = carve("m_dd", 55296 + 224, [128, 1], F32)
    w1sb = carve("m_w1s", 55296 + 256, [128, 1], F32)
    w2sb_ = carve("m_w2s", 55296 + 288, [128, 1], F32)
    bar = r.barrier()
    b_lg, b_lg2, b_mk1, b_mk2, b_t8 = B_(bar), B_(bar), B_(bar), B_(bar), B_(bar)
    b_m1, b_m2, b_dd, b_w1s, b_w2s = B_(bar), B_(bar), B_(bar), B_(bar), B_(bar)

    for tau in range(NTAU_):
        b = tau % 2
        sl = slice(tau * 512, (tau + 1) * 512)
        emit_ln(cx, hs, tau, b, _fm(d["x1"]))
        r.op("sync", lambda s, b=b, sl=sl: s.dma_start(
            out=_fm(d["xn_scr"])[:, :, sl], in_=xn_buf[:, b, :, :]),
            reads=[hs["b_xn"][b]], dma=True)
        for st in range(4):
            TT = tau * 4 + st
            fns = [(lambda t, b=b, c=c, st=st: t.matmul(
                PS[:, 4, 0:NE], xn_buf[:, b, c, st * 128:(st + 1) * 128],
                gate_sb[:, c, :], start=(c == 0), stop=(c == KC - 1))) for c in range(KC)]
            r.op("tensor", fns, reads=[hs["b_xn"][b], b_gate], writes=[bank[4]])
            r.op("vector", lambda v: v.tensor_copy(lgb[:], PS[:, 4, 0:NE]),
                 reads=[bank[4]], writes=[b_lg])
            r.op("vector", lambda v: v.reduce_max(m1b[:], lgb[:], AX.X),
                 reads=[b_lg], writes=[b_m1])
            r.op("vector", lambda v: v.tensor_single_scalar(
                mk1b[:], lgb[:], m1b[:, 0:1], ALU.is_equal),
                reads=[b_lg, b_m1], writes=[b_mk1])
            r.op("vector", lambda v: v.scalar_tensor_tensor(
                lg2b[:], mk1b[:], -1e30, lgb[:], ALU.mult, ALU.add),
                reads=[b_mk1, b_lg], writes=[b_lg2])
            r.op("vector", lambda v: v.reduce_max(m2b[:], lg2b[:], AX.X),
                 reads=[b_lg2], writes=[b_m2])
            r.op("vector", lambda v: v.tensor_single_scalar(
                mk2b[:], lg2b[:], m2b[:, 0:1], ALU.is_equal),
                reads=[b_lg2, b_m2], writes=[b_mk2])
            r.op("vector", lambda v: v.tensor_sub(ddb[:], m1b[:], m2b[:]),
                 reads=[b_m1, b_m2], writes=[b_dd])
            r.op("scalar", lambda s: s.activation(w1sb[:], ddb[:], AF.Sigmoid),
                 reads=[b_dd], writes=[b_w1s])
            r.op("vector", lambda v: v.tensor_scalar(
                w2sb_[:], w1sb[:], -1.0, 1.0, ALU.mult, ALU.add),
                reads=[b_w1s], writes=[b_w2s])
            r.op("vector", lambda v: v.tensor_scalar(
                tmp8b[:], mk1b[:], w1sb[:, 0:1], None, ALU.mult),
                reads=[b_mk1, b_w1s], writes=[b_t8])
            r.op("vector", lambda v, TT=TT: v.scalar_tensor_tensor(
                wdb[:, TT, :], mk2b[:], w2sb_[:, 0:1], tmp8b[:], ALU.mult, ALU.add),
                reads=[b_mk2, b_w2s, b_t8], writes=[b_wdb])
    t_m0done = r.tok("dsy")

    xn1t = carve("m_xn1t", 0, [128, 2, KC, 512])
    x1t = carve("m_x1t", 16384, [128, KC, 512])
    x2st = carve("m_x2st", 24576, [128, KC, 512])
    hth = carve("m_hth", 32768, [128, NIC, 512])
    w1s_t = carve("m_w1s_t", 65536, [128, 2, KC, 128])
    w3s_t = carve("m_w3s_t", 69632, [128, 2, KC, 128])
    w2sb = carve("m_w2sb", 73728, [128, 2, NIC, 512])
    accb_f = carve("m_accf", 139264, [128, 4, 1024], F32)
    accb_h = carve("m_acch", 155648, [128, 4, 1024])
    sgb = carve("m_sg", 163840, [128, 512])
    htmp = carve("m_ht", 164864, [128, 512])
    wbc = carve("m_wbc", 165888, [128, 512])
    wdrep = carve("m_wdr", 166912, [128, 128])
    bar = r.barrier()
    b_xn1 = [B_(bar), B_(bar)]
    b_x1t, b_x2st = B_(bar), B_(bar)
    b_hth = B_(bar)
    b_w1t = [B_(bar), B_(bar)]
    b_w3t = [B_(bar), B_(bar)]
    b_w2t = [B_(bar), B_(bar)]
    b_acc, b_accb = B_(bar), B_(bar)
    b_sg, b_ht, b_wbc, b_wdr = B_(bar), B_(bar), B_(bar), B_(bar)

    for tau in range(NTAU_):
        sl = slice(tau * 512, (tau + 1) * 512)
        tb = tau % 2
        if tau == 0:
            r.op("sync", lambda s, sl=sl: s.dma_start(
                out=xn1t[:, 0, :, :], in_=_fm(d["xn_scr"])[:, :, sl]),
                writes=[b_xn1[0]], extra=[t_m0done], dma=True)
        if tau + 1 < NTAU_:
            r.op("sync", lambda s, tau=tau: s.dma_start(
                out=xn1t[:, (tau + 1) % 2, :, :],
                in_=_fm(d["xn_scr"])[:, :, (tau + 1) * 512:(tau + 2) * 512]),
                writes=[b_xn1[(tau + 1) % 2]], extra=[t_m0done], dma=True)
        for e in range(NE):
            for st in range(4):
                TT = tau * 4 + st
                r.op("vector", lambda v, TT=TT, e=e: v.tensor_copy(
                    wdrep[:], bass.AP(wdb, TT * NE + e, [[NTT_ * NE, 128], [0, 128]])),
                    reads=[b_wdb], writes=[b_wdr])
                r.op("tensor", lambda t, st=st: t.matmul(
                    bass.AP(PS, 6 * 512 + st * 128, [[4096, 128], [1, 128]]),
                    wdrep[:], cx.ident_sb[:], start=True, stop=True),
                    reads=[b_wdr, cx.b_ident], writes=[bank[6]])
            r.op("scalar", lambda s: s.activation(wbc[:], psb(6), AF.Copy),
                 reads=[bank[6]], writes=[b_wbc])
            for hh in range(2):
                r.op("sync", lambda s, e=e, hh=hh: s.dma_start(
                    out=w2sb[:, hh, :, :],
                    in_=d["w2b"][e, hh].rearrange("p (ic dd) -> p ic dd", ic=NIC)),
                    writes=[b_w2t[hh]], extra=[t_cast], dma=True)
            for ic in range(NIC):
                s_ = ic % 2
                if ic < 2:
                    for wsl, wt, bb in ((w1s_t, d["w1b"], b_w1t), (w3s_t, d["w3b"], b_w3t)):
                        r.op("sync", lambda s, e=e, ic=ic, s_=s_, wsl=wsl, wt=wt: s.dma_start(
                            out=wsl[:, s_, :, :],
                            in_=wt[e, ic].rearrange("p (c i) -> p c i", c=KC)),
                            writes=[bb[s_]], extra=[t_cast], dma=True)
                bg, bu = s_ * 2, s_ * 2 + 1
                fns = [(lambda t, c=c, s_=s_, tb=tb: t.matmul(
                    PS[:, s_ * 2, :], w1s_t[:, s_, c, :], xn1t[:, tb, c, :],
                    start=(c == 0), stop=(c == KC - 1))) for c in range(KC)]
                r.op("tensor", fns, reads=[b_w1t[s_], b_xn1[tb]], writes=[bank[bg]])
                fns = [(lambda t, c=c, s_=s_, tb=tb: t.matmul(
                    PS[:, s_ * 2 + 1, :], w3s_t[:, s_, c, :], xn1t[:, tb, c, :],
                    start=(c == 0), stop=(c == KC - 1))) for c in range(KC)]
                r.op("tensor", fns, reads=[b_w3t[s_], b_xn1[tb]], writes=[bank[bu]])
                if ic + 2 < NIC:
                    for wsl, wt, bb in ((w1s_t, d["w1b"], b_w1t), (w3s_t, d["w3b"], b_w3t)):
                        r.op("sync", lambda s, e=e, ic=ic, s_=s_, wsl=wsl, wt=wt: s.dma_start(
                            out=wsl[:, s_, :, :],
                            in_=wt[e, ic + 2].rearrange("p (c i) -> p c i", c=KC)),
                            writes=[bb[s_]], extra=[t_cast], dma=True)
                r.op("scalar", lambda s, bg=bg: s.activation(sgb[:], psb(bg), AF.Silu),
                     reads=[bank[bg]], writes=[b_sg])
                r.op("vector", lambda v, bu=bu: v.tensor_tensor(
                    htmp[:], sgb[:], psb(bu), ALU.mult),
                    reads=[b_sg, bank[bu]], writes=[b_ht])
                r.op("vector", lambda v, ic=ic: v.tensor_tensor(
                    hth[:, ic, :], htmp[:], wbc[:], ALU.mult),
                    reads=[b_ht, b_wbc], writes=[b_hth])
            for hh in range(2):
                for st in range(4):
                    pw = 4 + (st % 2)
                    fns = [(lambda t, ic=ic, st=st, hh=hh, pw=pw: t.matmul(
                        psb(pw), hth[:, ic, st * 128:(st + 1) * 128], w2sb[:, hh, ic, :],
                        start=(ic == 0), stop=(ic == NIC - 1))) for ic in range(NIC)]
                    r.op("tensor", fns, reads=[b_hth, b_w2t[hh]], writes=[bank[pw]])
                    if e == 0:
                        r.op("vector", lambda v, st=st, hh=hh, pw=pw: v.tensor_copy(
                            accb_f[:, st, hh * 512:(hh + 1) * 512], psb(pw)),
                            reads=[bank[pw]], writes=[b_acc])
                    else:
                        r.op("vector", lambda v, st=st, hh=hh, pw=pw: v.tensor_tensor(
                            accb_f[:, st, hh * 512:(hh + 1) * 512], psb(pw),
                            accb_f[:, st, hh * 512:(hh + 1) * 512], ALU.add),
                            reads=[bank[pw]], writes=[b_acc])
        r.op("vector", lambda v: v.tensor_copy(accb_h[:], accb_f[:]),
             reads=[b_acc], writes=[b_accb])
        r.op("sync", lambda s, sl=sl: s.dma_start(
            out=x1t[:], in_=_fm(d["x1"])[:, :, sl]), writes=[b_x1t], dma=True)
        for st in range(4):
            pt = 6 + (st % 2)
            fns = [(lambda t, st=st, dc=dc, pt=pt: t.transpose(
                PS[:, pt, :].bitcast(BF16)[:, dc * 128:(dc + 1) * 128],
                accb_h[:, st, dc * 128:(dc + 1) * 128], cx.ident_sb[:])) for dc in range(KC)]
            r.op("tensor", fns, reads=[b_accb, cx.b_ident], writes=[bank[pt]])
            r.op("vector", lambda v, st=st, pt=pt: v.tensor_tensor(
                x2st[:, :, st * 128:(st + 1) * 128], x1t[:, :, st * 128:(st + 1) * 128],
                PS[:, pt, :].bitcast(BF16).rearrange("p (c t) -> p c t", c=KC), ALU.add),
                reads=[bank[pt], b_x1t], writes=[b_x2st])
        r.op("sync", lambda s, sl=sl: s.dma_start(
            out=_fm(d["x2"])[:, :, sl], in_=x2st[:]),
            reads=[b_x2st], dma=True)


def emit_l1attn(cx, d):
    nc, r = cx.nc, cx.r
    carve, psb, psap = cx.carve, cx.psb, cx.psap
    PS, bank = cx.PS, cx.bank
    B_ = Buf

    wo_sb = carve("c_wo", 0, [128, KC, D])
    vsb = carve("c_vsb", 16384, [128, NTT, 1168])
    wq_sb = carve("c_wq", 91136, [128, 128])
    wk_sb = carve("c_wk", 91392, [128, 128])
    wv_sb = carve("c_wv", 91648, [128, 128])
    mask_sb = carve("c_mask", 91904, [128, 4, 512])
    PB = 96000
    bar = r.barrier()
    b_wq, b_wk, b_wv, b_wo, b_mask, b_vsb = B_(bar), B_(bar), B_(bar), B_(bar), B_(bar), B_(bar)

    r.op("gpsimd", lambda g: g.dma_start(out=wq_sb[:], in_=d["wq2"]), writes=[b_wq], dma=True)
    r.op("gpsimd", lambda g: g.dma_start(out=wk_sb[:], in_=d["wk2"]), writes=[b_wk], dma=True)
    r.op("gpsimd", lambda g: g.dma_start(out=wv_sb[:], in_=d["wv2"]), writes=[b_wv], dma=True)
    r.op("gpsimd", lambda g: g.dma_start(
        out=wo_sb[:], in_=d["wo"].rearrange("(kc p) m -> p kc m", p=128)), writes=[b_wo], dma=True)
    r.op("sync", lambda s: s.dma_start(
        out=mask_sb[:], in_=d["cmask"].rearrange("(i p) q -> p i q", p=128)),
        writes=[b_mask], dma=True)
    r.op("gpsimd", lambda g: g.memset(
        bass.AP(vsb, 64, [[NTT * 1168, 128], [1168, NTT], [65, 16]]), 1.0), writes=[b_vsb])
    r.op("gpsimd", lambda g: g.memset(
        bass.AP(vsb, 1040, [[NTT * 1168, 128], [1168, NTT], [1, 128]]), 0.0), writes=[b_vsb])

    hs = mk_ln_bufs(cx, PB, "c_")
    xn_buf = hs["xn_buf"]
    qk_stage = carve("c_qks", PB + 55296, [128, 2, 512])
    bar = r.barrier()
    b_qks = [B_(bar), B_(bar)]

    for tau in range(NTAU):
        b = tau % 2
        sl = slice(tau * 512, (tau + 1) * 512)
        emit_ln(cx, hs, tau, b, _fm(d["x2"]))
        for which, wsb, bw, dst in (("q", wq_sb, b_wq, d["q_scr"]), ("k", wk_sb, b_wk, d["k_scr"])):
            for c in range(KC):
                pb = c % 2
                r.op("tensor", lambda t, b=b, c=c, wsb=wsb, pb=pb: t.matmul(
                    psb(pb), wsb[:], xn_buf[:, b, c, :], start=True, stop=True),
                    reads=[hs["b_xn"][b], bw], writes=[bank[pb]])
                r.op("scalar", lambda s, pb=pb, c=c: s.activation(
                    qk_stage[:, c % 2, :], psb(pb), AF.Copy),
                    reads=[bank[pb]], writes=[b_qks[c % 2]])
                r.op("sync", lambda s, c=c, dst=dst, sl=sl: s.dma_start(
                    out=dst[c * 128:(c + 1) * 128, sl], in_=qk_stage[:, c % 2, :]),
                    reads=[b_qks[c % 2]], dma=True)
        for sub in range(4):
            tt = tau * 4 + sub
            for c in range(KC):
                pb = 2 + (c % 2)
                r.op("tensor", lambda t, b=b, c=c, sub=sub, pb=pb: t.matmul(
                    PS[:, pb, 0:128], xn_buf[:, b, c, sub * 128:(sub + 1) * 128], wv_sb[:],
                    start=True, stop=True),
                    reads=[hs["b_xn"][b], b_wv], writes=[bank[pb]])
                r.op("vector", lambda v, tt=tt, c=c, pb=pb: v.tensor_copy(
                    bass.AP(vsb, tt * 1168 + (2 * c) * 65, [[NTT * 1168, 128], [65, 2], [1, 64]]),
                    bass.AP(PS, pb * 512, [[4096, 128], [64, 2], [1, 64]])),
                    reads=[bank[pb]], writes=[b_vsb])
    t_qkdone = r.tok("dsy")

    emit_attn_phase(cx, d, "c_", PB, vsb, b_vsb, causal=True,
                    mask_sb=mask_sb, b_mask=b_mask, t_qkdone=t_qkdone)
    t_attdone = r.tok("dsy")

    at_buf = carve("c_at", PB + 0, [128, 2, KC, 512])
    xt4_buf = carve("c_xt4", PB + 16384, [128, 2, KC, 512])
    x3_stage = carve("c_x3s", PB + 32768, [128, 2, KC, 512])
    bar = r.barrier()
    b_at = [B_(bar), B_(bar)]
    b_xt4 = [B_(bar), B_(bar)]
    b_x3 = [B_(bar), B_(bar)]
    for tau in range(NTAU):
        b = tau % 2
        sl = slice(tau * 512, (tau + 1) * 512)
        r.op("sync", lambda s, b=b, sl=sl: s.dma_start(
            out=at_buf[:, b, :, :], in_=_fm(d["attn_scr"])[:, :, sl]),
            writes=[b_at[b]], extra=[t_attdone], dma=True)
        r.op("sync", lambda s, b=b, sl=sl: s.dma_start(
            out=xt4_buf[:, b, :, :], in_=_fm(d["x2"])[:, :, sl]),
            writes=[b_xt4[b]], dma=True)
        for f in range(KC):
            pb = f % 2
            fns = [(lambda t, b=b, c=c, f=f, pb=pb: t.matmul(
                psb(pb), wo_sb[:, c, f * 128:(f + 1) * 128], at_buf[:, b, c, :],
                start=(c == 0), stop=(c == KC - 1))) for c in range(KC)]
            r.op("tensor", fns, reads=[b_at[b], b_wo], writes=[bank[pb]])
            r.op("vector", lambda v, b=b, f=f, pb=pb: v.tensor_tensor(
                x3_stage[:, b, f, :], xt4_buf[:, b, f, :], psb(pb), ALU.add),
                reads=[bank[pb], b_xt4[b]], writes=[b_x3[b]])
        r.op("sync", lambda s, b=b, sl=sl: s.dma_start(
            out=_fm(d["x3"])[:, :, sl], in_=x3_stage[:, b, :, :]),
            reads=[b_x3[b]], dma=True)


def emit_l1ffn(cx, d):
    nc, r = cx.nc, cx.r
    carve, psb, psbf = cx.carve, cx.psb, cx.psbf
    PS, bank = cx.PS, cx.bank
    B_ = Buf
    U16 = mybir.dt.uint16

    wgT = carve("f_wgT", 0, [128, KC, I2])
    wuT = carve("f_wuT", 45056, [128, KC, I2])
    wdT = carve("f_wdT", 90112, [128, NN, D])
    FB = 135168
    bar = r.barrier()
    b_wgT, b_wuT, b_wdT = B_(bar), B_(bar), B_(bar)

    wp_buf = carve("f_wp", FB + 0, [128, 2816])
    ab_buf = carve("f_ab", FB + 5632, [128, 2816])
    accA = carve("f_accA", FB + 11264, [128, 4096])
    accB = carve("f_accB", FB + 19456, [128, 4096])
    amx = carve("f_amx", FB + 27648, [128, 176], F32)
    scb = carve("f_scb", FB + 28416, [128, 176], F32)
    rscb = carve("f_rscb", FB + 29184, [128, 176], F32)
    bar = r.barrier()
    b_wp, b_ab, b_aA, b_aB = B_(bar), B_(bar), B_(bar), B_(bar)
    b_amx, b_sc, b_rsc = B_(bar), B_(bar), B_(bar)

    U32 = mybir.dt.uint32

    def emit_quant(Xap, b_X, nblk):
        ne = nblk * 16
        X3 = Xap.rearrange("p (nb s) -> p nb s", s=16)
        r.op("vector", lambda v: v.tensor_reduce(
            amx[:, 0:nblk], X3, AX.X, ALU.max, apply_absolute_value=True),
            reads=[b_X], writes=[b_amx])
        r.op("vector", lambda v: v.tensor_scalar(
            scb[:, 0:nblk], amx[:, 0:nblk], 1.0 / 6.0, 1e-12, ALU.mult, ALU.max),
            reads=[b_amx], writes=[b_sc])
        r.op("vector", lambda v: v.reciprocal(rscb[:, 0:nblk], scb[:, 0:nblk]),
             reads=[b_sc], writes=[b_rsc])
        r.op("scalar", lambda s: s.activation(ab_buf[:, 0:ne], Xap, AF.Abs),
             reads=[b_X], writes=[b_ab])
        rsc_b = bass.AP(rscb, 0, [[176, 128], [1, nblk], [0, 16]])
        sc_b = bass.AP(scb, 0, [[176, 128], [1, nblk], [0, 16]])
        r.op("vector", lambda v: v.tensor_tensor(
            ab_buf[:, 0:ne].rearrange("p (nb s) -> p nb s", s=16),
            ab_buf[:, 0:ne].rearrange("p (nb s) -> p nb s", s=16), rsc_b, ALU.mult),
            reads=[b_rsc], writes=[b_ab])
        a = ab_buf[:, 0:ne]
        r.op("vector", lambda v: v.tensor_single_scalar(
            accA[:, 0:ne], a, 0.25, ALU.is_gt), reads=[b_ab], writes=[b_aA])
        for th in (0.75, 1.25, 1.75):
            r.op("vector", lambda v, th=th: v.scalar_tensor_tensor(
                accA[:, 0:ne], a, th, accA[:, 0:ne], ALU.is_gt, ALU.add),
                reads=[b_ab], writes=[b_aA])
        r.op("vector", lambda v: v.tensor_single_scalar(
            accB[:, 0:ne], a, 2.5, ALU.is_gt), reads=[b_ab], writes=[b_aB])
        for th in (3.5, 5.0, 5.0):
            r.op("vector", lambda v, th=th: v.scalar_tensor_tensor(
                accB[:, 0:ne], a, th, accB[:, 0:ne], ALU.is_gt, ALU.add),
                reads=[b_ab], writes=[b_aB])
        r.op("vector", lambda v: v.scalar_tensor_tensor(
            accA[:, 0:ne], accA[:, 0:ne], 0.5, accB[:, 0:ne], ALU.mult, ALU.add),
            reads=[b_aB], writes=[b_aA])
        r.op("vector", lambda v: v.tensor_tensor(
            accA[:, 0:ne].rearrange("p (nb s) -> p nb s", s=16),
            accA[:, 0:ne].rearrange("p (nb s) -> p nb s", s=16), sc_b, ALU.mult),
            reads=[b_sc], writes=[b_aA])
        r.op("vector", lambda v: v.tensor_single_scalar(
            ab_buf[:, 0:ne].bitcast(U32), Xap.bitcast(U32), -2147450880,
            ALU.bitwise_and),
            reads=[b_X], writes=[b_ab])
        r.op("vector", lambda v: v.tensor_tensor(
            Xap.bitcast(U32), accA[:, 0:ne].bitcast(U32), ab_buf[:, 0:ne].bitcast(U32),
            ALU.bitwise_or),
            reads=[b_aA, b_ab], writes=[b_X])

    for w_in, wT, b_wT in ((d["wg"], wgT, b_wgT), (d["wu"], wuT, b_wuT)):
        for pp in range(NN // 2):
            r.op("gpsimd", lambda g, w_in=w_in, pp=pp: g.dma_start(
                out=wp_buf[:, 0:2048].rearrange("p (n k) -> p n k", n=2),
                in_=w_in.rearrange("(n p) k -> p n k", p=128)[:, 2 * pp:2 * pp + 2, :]),
                writes=[b_wp], dma=True)
            emit_quant(wp_buf[:, 0:2048], b_wp, 128)
            for n2 in range(2):
                pt = 6 + n2
                fns = [(lambda t, n2=n2, kcc=kcc, pt=pt: t.transpose(
                    psbf(pt)[:, kcc * 128:(kcc + 1) * 128],
                    wp_buf[:, n2 * 1024 + kcc * 128:n2 * 1024 + (kcc + 1) * 128],
                    cx.ident_sb[:])) for kcc in range(KC)]
                r.op("tensor", fns, reads=[b_wp, cx.b_ident], writes=[bank[pt]])
                r.op("vector", lambda v, n2=n2, pp=pp, pt=pt, wT=wT: v.tensor_copy(
                    wT[:, :, (2 * pp + n2) * 128:(2 * pp + n2 + 1) * 128],
                    psbf(pt).rearrange("p (c t) -> p c t", c=KC)),
                    reads=[bank[pt]], writes=[b_wT])
    for dc in range(KC):
        r.op("gpsimd", lambda g, dc=dc: g.dma_start(
            out=wp_buf[:],
            in_=d["wd"].rearrange("(dd p) k -> p dd k", p=128)[:, dc, :]),
            writes=[b_wp], dma=True)
        emit_quant(wp_buf[:], b_wp, 176)
        for grp, n0 in ((8, 0), (8, 8), (6, 16)):
            pt = 6 + (n0 // 8) % 2
            fns = [(lambda t, icc=icc, pt=pt: t.transpose(
                psbf(pt)[:, (icc % 8) * 128:((icc % 8) + 1) * 128],
                wp_buf[:, icc * 128:(icc + 1) * 128],
                cx.ident_sb[:])) for icc in range(n0, n0 + grp)]
            r.op("tensor", fns, reads=[b_wp, cx.b_ident], writes=[bank[pt]])
            r.op("vector", lambda v, grp=grp, n0=n0, dc=dc, pt=pt: v.tensor_copy(
                wdT[:, n0:n0 + grp, dc * 128:(dc + 1) * 128],
                psbf(pt).rearrange("p (c t) -> p c t", c=KC)[:, 0:grp, :]),
                reads=[bank[pt]], writes=[b_wdT])

    sg_buf = carve("f_sg", FB + 29952, [128, 512])
    xt_buf = carve("f_xt", FB + 30976, [128, 1, KC, 512])
    h_tm1 = carve("f_htm1", FB + 39168, [128, 2816])
    xn_buf = carve("f_xn", FB + 44800, [128, KC, 512])
    hq_fm = carve("f_hq", FB + 44800, [128, NN, 128])
    sq_buf = carve("f_sq", FB + 11264, [128, KC, 512])
    tf_buf = carve("f_tf", FB + 19456, [128, KC, 512])
    mbuf = carve("f_mb", FB + 52992, [128, 512], F32)
    msqb = carve("f_ms", FB + 55040, [128, 512], F32)
    varb = carve("f_va", FB + 57088, [128, 512], F32)
    xtm_sb = carve("f_xtm", FB + 59136, [128, 2048])
    xq_fm = carve("f_xq", FB + 63232, [128, KC, 512])
    x4_sub = carve("f_x4", FB + 71424, [128, KC, 128])
    out_tm = carve("f_otm", FB + 52992, [128, 1024], F32)
    bar = r.barrier()
    b_m = B_(bar)
    b_xn = B_(bar)
    b_xt = B_(bar)
    hs = {
        "xt_buf": xt_buf,
        "sq_buf": sq_buf, "tf_buf": tf_buf,
        "xn_buf": xn_buf, "xn_ap": (lambda b: xn_buf[:]),
        "mbuf": mbuf, "msqb": msqb, "varb": varb,
        "b_xt": [b_xt, b_xt], "b_sq": b_aA, "b_tf": b_aB,
        "b_xn": b_xn, "b_m": b_m, "b_msq": b_m, "b_var": B_(bar),
    }
    b_xtm, b_xq = B_(bar), B_(bar)
    b_htm = [b_wp, B_(bar)]
    h_tm = [wp_buf, h_tm1]
    b_hqfm = b_xn
    b_sg = B_(bar)
    b_x4 = B_(bar)
    b_otm = b_m

    def gate_up(sub, tb):
        for ns in range(6):
            n0 = ns * 512
            nw = min(512, I2 - n0)
            ga = (ns % 2) * 2
            fns = [(lambda t, c=c, n0=n0, nw=nw, ga=ga, sub=sub: t.matmul(
                PS[:, ga, 0:nw], xq_fm[:, c, sub * 128:(sub + 1) * 128],
                wgT[:, c, n0:n0 + nw],
                start=(c == 0), stop=(c == KC - 1))) for c in range(KC)]
            r.op("tensor", fns, reads=[b_xq, b_wgT], writes=[bank[ga]])
            fns = [(lambda t, c=c, n0=n0, nw=nw, ga=ga, sub=sub: t.matmul(
                PS[:, ga + 1, 0:nw], xq_fm[:, c, sub * 128:(sub + 1) * 128],
                wuT[:, c, n0:n0 + nw],
                start=(c == 0), stop=(c == KC - 1))) for c in range(KC)]
            r.op("tensor", fns, reads=[b_xq, b_wuT], writes=[bank[ga + 1]])
            r.op("scalar", lambda s, nw=nw, ga=ga: s.activation(
                sg_buf[:, 0:nw], PS[:, ga, 0:nw], AF.Silu),
                reads=[bank[ga]], writes=[b_sg])
            r.op("vector", lambda v, n0=n0, nw=nw, ga=ga, tb=tb: v.tensor_tensor(
                h_tm[tb][:, n0:n0 + nw], sg_buf[:, 0:nw], PS[:, ga + 1, 0:nw], ALU.mult),
                reads=[b_sg, bank[ga + 1]], writes=[b_htm[tb]])

    def tail(sub, tb, tau):
        emit_quant(h_tm[tb][:], b_htm[tb], 176)
        for grp, n0 in ((8, 0), (8, 8), (6, 16)):
            pt = 6 + (n0 // 8) % 2
            fns = [(lambda t, icc=icc, pt=pt, tb=tb: t.transpose(
                psbf(pt)[:, (icc % 8) * 128:((icc % 8) + 1) * 128],
                h_tm[tb][:, icc * 128:(icc + 1) * 128], cx.ident_sb[:]))
                for icc in range(n0, n0 + grp)]
            r.op("tensor", fns, reads=[b_htm[tb], cx.b_ident], writes=[bank[pt]])
            r.op("scalar", lambda s, grp=grp, n0=n0, pt=pt: s.activation(
                hq_fm[:, n0:n0 + grp, :],
                psbf(pt).rearrange("p (c t) -> p c t", c=KC)[:, 0:grp, :], AF.Copy),
                reads=[bank[pt]], writes=[b_hqfm])
        for f in range(KC):
            pw = 4 + (f // 4)
            col = (f % 4) * 128
            fns = [(lambda t, icc=icc, f=f, pw=pw, col=col: t.matmul(
                bass.AP(PS, pw * 512 + col, [[4096, 128], [1, 128]]),
                wdT[:, icc, f * 128:(f + 1) * 128], hq_fm[:, icc, :],
                start=(icc == 0), stop=(icc == NN - 1))) for icc in range(NN)]
            r.op("tensor", fns, reads=[b_wdT, b_hqfm], writes=[bank[pw]])
        r.op("vector", lambda v, sub=sub: v.tensor_tensor(
            x4_sub[:], xt_buf[:, 0, :, sub * 128:(sub + 1) * 128],
            PS[:, 4:6, :].rearrange("p a b -> p (a b)").rearrange(
                "p (c t) -> p c t", c=KC), ALU.add),
            reads=[bank[4], bank[5], b_xt], writes=[b_x4])
        fns = [(lambda t, c=c: t.transpose(
            psbf(6)[:, c * 128:(c + 1) * 128],
            x4_sub[:, c, :], cx.ident_sb[:])) for c in range(KC)]
        r.op("tensor", fns, reads=[b_x4, cx.b_ident], writes=[bank[6]])
        r.op("scalar", lambda s: s.activation(out_tm[:], psbf(6), AF.Copy),
             reads=[bank[6]], writes=[b_otm])
        r.op("sync", lambda s, tau=tau, sub=sub: s.dma_start(
            out=d["out"][tau * 512 + sub * 128:tau * 512 + (sub + 1) * 128, :],
            in_=out_tm[:]),
            reads=[b_otm], dma=True)

    for tau in range(NTAU):
        emit_ln(cx, hs, tau, 0, _fm(d["x3"]))
        for hf in range(2):
            for s2 in range(2):
                sub = hf * 2 + s2
                fns = [(lambda t, c=c, sub=sub: t.transpose(
                    psbf(7)[:, c * 128:(c + 1) * 128],
                    xn_buf[:, c, sub * 128:(sub + 1) * 128], cx.ident_sb[:]))
                    for c in range(KC)]
                r.op("tensor", fns, reads=[b_xn, cx.b_ident], writes=[bank[7]])
                r.op("scalar", lambda s, s2=s2: s.activation(
                    xtm_sb[:, s2 * 1024:(s2 + 1) * 1024], psbf(7), AF.Copy),
                    reads=[bank[7]], writes=[b_xtm])
            emit_quant(xtm_sb[:], b_xtm, 128)
            for s2 in range(2):
                sub = hf * 2 + s2
                fns = [(lambda t, c=c, s2=s2: t.transpose(
                    psbf(7)[:, c * 128:(c + 1) * 128],
                    xtm_sb[:, s2 * 1024 + c * 128:s2 * 1024 + (c + 1) * 128],
                    cx.ident_sb[:])) for c in range(KC)]
                r.op("tensor", fns, reads=[b_xtm, cx.b_ident], writes=[bank[7]])
                r.op("scalar", lambda s, sub=sub: s.activation(
                    bass.AP(xq_fm, sub * 128, [[KC * 512, 128], [512, KC], [1, 128]]),
                    psbf(7).rearrange("p (c t) -> p c t", c=KC), AF.Copy),
                    reads=[bank[7]], writes=[b_xq])
        gate_up(0, 0)
        for sub in range(1, 4):
            gate_up(sub, sub % 2)
            tail(sub - 1, (sub - 1) % 2, tau)
        tail(3, 1, tau)

def build_fused():
    nc = bass.Bass("TRN2", target_bir_lowering=False)
    d = {}
    d["x"] = nc.dram_tensor("x", [T, D], F32, kind="ExternalInput").ap()
    d["wqkv"] = nc.dram_tensor("wqkv", [D, 3 * D], F32, kind="ExternalInput").ap()
    d["bqkv"] = nc.dram_tensor("bqkv", [1, 3 * D], F32, kind="ExternalInput").ap()
    d["wproj"] = nc.dram_tensor("wproj", [D, D], F32, kind="ExternalInput").ap()
    d["bproj"] = nc.dram_tensor("bproj", [1, D], F32, kind="ExternalInput").ap()
    d["gate"] = nc.dram_tensor("gate", [D, 8], F32, kind="ExternalInput").ap()
    d["w1"] = nc.dram_tensor("w1", [8, 32, 128, 1024], F32, kind="ExternalInput").ap()
    d["w3"] = nc.dram_tensor("w3", [8, 32, 128, 1024], F32, kind="ExternalInput").ap()
    d["w2"] = nc.dram_tensor("w2", [8, 2, 128, 32 * 512], F32, kind="ExternalInput").ap()
    d["wq2"] = nc.dram_tensor("wq2", [128, 128], F32, kind="ExternalInput").ap()
    d["wk2"] = nc.dram_tensor("wk2", [128, 128], F32, kind="ExternalInput").ap()
    d["wv2"] = nc.dram_tensor("wv2", [128, 128], F32, kind="ExternalInput").ap()
    d["wo"] = nc.dram_tensor("wo", [D, D], F32, kind="ExternalInput").ap()
    d["wg"] = nc.dram_tensor("wg", [I2, D], F32, kind="ExternalInput").ap()
    d["wu"] = nc.dram_tensor("wu", [I2, D], F32, kind="ExternalInput").ap()
    d["wd"] = nc.dram_tensor("wd", [D, I2], F32, kind="ExternalInput").ap()
    d["cones"] = nc.dram_tensor("cones", [128, 512], BF16, kind="ExternalInput").ap()
    d["conesf"] = nc.dram_tensor("conesf", [1, 512], F32, kind="ExternalInput").ap()
    d["cident"] = nc.dram_tensor("cident", [128, 128], BF16, kind="ExternalInput").ap()
    d["cmask"] = nc.dram_tensor("cmask", [512, 512], BF16, kind="ExternalInput").ap()

    d["w1b"] = nc.dram_tensor("w1b", [8, 32, 128, 1024], BF16).ap()
    d["w3b"] = nc.dram_tensor("w3b", [8, 32, 128, 1024], BF16).ap()
    d["w2b"] = nc.dram_tensor("w2b", [8, 2, 128, 32 * 512], BF16).ap()
    d["xfm_scr"] = nc.dram_tensor("xfm_scr", [D, T], BF16).ap()
    d["q_scr"] = nc.dram_tensor("q_scr", [D, T], BF16).ap()
    d["k_scr"] = nc.dram_tensor("k_scr", [D, T], BF16).ap()
    d["attn_scr"] = nc.dram_tensor("attn_scr", [D, T], BF16).ap()
    d["x1"] = nc.dram_tensor("x1", [D, T], BF16).ap()
    d["xn_scr"] = nc.dram_tensor("xn_scr", [D, T], BF16).ap()
    d["x2"] = nc.dram_tensor("x2", [D, T], BF16).ap()
    d["x3"] = nc.dram_tensor("x3", [D, T], BF16).ap()
    d["out"] = nc.dram_tensor("out", [T, D], F32, kind="ExternalOutput").ap()

    r = Rec()
    with contextlib.ExitStack() as es:
        cx = Ctx(nc, r, es)
        cx.load_consts(d["cones"], d["conesf"], d["cident"])
        t_cast = emit_precast(cx, d)
        emit_l0attn(cx, d)
        emit_l0moe(cx, d, t_cast=t_cast)
        emit_l1attn(cx, d)
        emit_l1ffn(cx, d)
        r._emit("sync", lambda s: s.nop(), [r.tok("dsy"), r.tok("dgp")], None, 1)
        r.build(nc)
    return nc


_cache = {}


def _consts():
    return dict(
        cones=np.ones((128, 512), dtype=BF),
        conesf=np.ones((1, 512), dtype=np.float32),
        cident=np.eye(128, dtype=BF),
    )


def _blkdiag(W):
    z = np.zeros((128, 128), np.float32)
    z[0:64, 0:64] = W.T
    z[64:128, 64:128] = W.T
    return z


def _causal_mask():
    j = np.arange(512)[None, :]
    m = np.zeros((512, 512), np.float32)
    for i in range(4):
        p = np.arange(128)[:, None]
        m[i * 128:(i + 1) * 128, :] = (j >= p + 128 * i)
    return m.astype(BF)


def kernel(**inputs):
    x = np.asarray(inputs["L0_attn_x"], dtype=np.float32)
    c = _consts()
    if "fused" not in _cache:
        _cache["fused"] = build_fused()
    common = dict(
        wqkv=np.ascontiguousarray(inputs["L0_attn_w_qkv"], dtype=np.float32),
        bqkv=np.ascontiguousarray(inputs["L0_attn_b_qkv"], dtype=np.float32).reshape(1, -1),
        wproj=np.ascontiguousarray(inputs["L0_attn_w_proj"], dtype=np.float32),
        bproj=np.ascontiguousarray(inputs["L0_attn_b_proj"], dtype=np.float32).reshape(1, -1),
        gate=np.ascontiguousarray(inputs["L0_ffn_gate"], dtype=np.float32),
        w1=np.ascontiguousarray(
            np.asarray(inputs["L0_ffn_experts_w1"], np.float32)
            .reshape(8, 8, 128, 32, 128).transpose(0, 3, 2, 1, 4).reshape(8, 32, 128, 1024)),
        w3=np.ascontiguousarray(
            np.asarray(inputs["L0_ffn_experts_w3"], np.float32)
            .reshape(8, 8, 128, 32, 128).transpose(0, 3, 2, 1, 4).reshape(8, 32, 128, 1024)),
        w2=np.ascontiguousarray(
            np.asarray(inputs["L0_ffn_experts_w2"], np.float32)
            .reshape(8, 32, 128, 2, 512).transpose(0, 3, 2, 1, 4).reshape(8, 2, 128, 32 * 512)),
        wq2=_blkdiag(np.asarray(inputs["L1_attn_Wq"], np.float32)),
        wk2=_blkdiag(np.asarray(inputs["L1_attn_Wk"], np.float32)),
        wv2=_blkdiag(np.asarray(inputs["L1_attn_Wv"], np.float32)),
        wo=np.ascontiguousarray(np.asarray(inputs["L1_attn_Wo"], np.float32).T),
        wg=np.ascontiguousarray(inputs["L1_ffn_w_gate"], dtype=np.float32),
        wu=np.ascontiguousarray(inputs["L1_ffn_w_up"], dtype=np.float32),
        wd=np.ascontiguousarray(inputs["L1_ffn_w_down"], dtype=np.float32),
        cmask=_causal_mask(),
        **c,
    )
    in_maps = [dict(x=np.ascontiguousarray(x[core]), **common) for core in range(8)]
    res = run_bass_kernel_spmd(_cache["fused"], in_maps, core_ids=list(range(8)))
    return np.stack([rr["out"] for rr in res.results])
